# version 11 (final)
# baseline (speedup 1.0000x reference)
import numpy as np
import ml_dtypes
import concourse.bass as bass
import concourse.mybir as mybir
from concourse.bass_utils import run_bass_kernel_spmd

F32 = mybir.dt.float32
BF16 = mybir.dt.bfloat16
AF = mybir.ActivationFunctionType
ALU = mybir.AluOpType
AXX = mybir.AxisListType.X

NCORES = 8
S = 16384
T = 2048
HALO = 256
TH = T + HALO
NT = TH // 128
NOT = T // 128
D = 1024
NDC = 8
H = 16
DH = 64
LH = 8
LD = 512
NL = 64
NQG = 8
M_EXP = 8.0
EPS = 1e-5
HID = 4096
NHC = HID // 128
KSLICE = 512
NK = T // KSLICE
NSL = NK * NCORES

PRJ_N = [36] + [24] * 7
PRJ_TOT = [0]
for _n in PRJ_N:
    PRJ_TOT.append(PRJ_TOT[-1] + _n)


def pairs_of(g):
    return [0, 1] if g == 0 else [g + 1]


def build():
    nc = bass.Bass("TRN2", target_bir_lowering=False, debug=False)
    xdr = nc.dram_tensor("x", [TH, D], F32, kind="ExternalInput").ap()
    wq = nc.dram_tensor("wq", [D, D], F32, kind="ExternalInput").ap()
    wk = nc.dram_tensor("wk", [D, D], F32, kind="ExternalInput").ap()
    wv = nc.dram_tensor("wv", [D, D], F32, kind="ExternalInput").ap()
    wgq = nc.dram_tensor("wgq", [D, LD], F32, kind="ExternalInput").ap()
    wgk = nc.dram_tensor("wgk", [D, LD], F32, kind="ExternalInput").ap()
    wgv = nc.dram_tensor("wgv", [D, LD], F32, kind="ExternalInput").ap()
    routd = nc.dram_tensor("rout", [LD, D], F32, kind="ExternalInput").ap()
    woutd = nc.dram_tensor("wout", [D, D], F32, kind="ExternalInput").ap()
    gbdd = nc.dram_tensor("gbd", [D, 32], F32, kind="ExternalInput").ap()
    gwd = nc.dram_tensor("gw", [D, 8], F32, kind="ExternalInput").ap()
    gbias = nc.dram_tensor("gbias", [1, 8], F32, kind="ExternalInput").ap()
    w1d = nc.dram_tensor("w1", [D, HID], F32, kind="ExternalInput").ap()
    b1d = nc.dram_tensor("b1col", [128, NHC], F32, kind="ExternalInput").ap()
    w2d = nc.dram_tensor("w2", [HID, D], F32, kind="ExternalInput").ap()
    b2d = nc.dram_tensor("b2row", [1, D], F32, kind="ExternalInput").ap()
    onehotd = nc.dram_tensor("onehot", [128, 8], F32, kind="ExternalInput").ap()
    identd = nc.dram_tensor("ident", [128, 128], BF16, kind="ExternalInput").ap()
    exld = nc.dram_tensor("exl", [16, NDC * 128], BF16, kind="ExternalInput").ap()
    exrd = nc.dram_tensor("exr", [8, 4 * 128], BF16, kind="ExternalInput").ap()
    masksd = nc.dram_tensor("masks", [NQG, 4, 128, 256], BF16,
                            kind="ExternalInput").ap()
    latmaskd = nc.dram_tensor("latmask", [NQG, 128, 256], BF16,
                              kind="ExternalInput").ap()
    ydr = nc.dram_tensor("y", [T, D], F32, kind="ExternalOutput").ap()

    xlnT_dram = nc.dram_tensor("xlnT_dram", [NT, NDC, 128, 128], BF16)
    xres_dram = nc.dram_tensor("xres_dram", [T, D], F32)
    cc_pool_in = nc.dram_tensor("cc_pool_in", [8, D], F32)
    cc_pool_out = nc.dram_tensor("cc_pool_out", [NL, D], F32)
    cc_x_inh = [nc.dram_tensor(f"cc_x_in{h_}", [D, T // 2], BF16)
                for h_ in range(2)]
    cc_x_outh = [nc.dram_tensor(f"cc_x_out{h_}", [NCORES, D, T // 2], BF16)
                 for h_ in range(2)]
    cc_m_in = nc.dram_tensor("cc_m_in", [T, 8], F32)
    cc_m_outh = [nc.dram_tensor(f"cc_m_out{h_}", [NCORES, T // 2, 8], F32)
                 for h_ in range(2)]
    rs_in = [nc.dram_tensor(f"rs_in{k}", [NCORES, KSLICE // 2, D], F32)
             for k in range(2 * NK)]
    rs_out = [nc.dram_tensor(f"rs_out{k}", [KSLICE // 2, D], F32)
              for k in range(2 * NK)]

    from contextlib import ExitStack
    es = ExitStack()
    with es:
        MAPBASE = 16512
        nc.alloc_sbuf_tensor("reserved_map", [128, 51760], F32)

        def at(name, shape, dt, off):
            return nc.alloc_sbuf_tensor_at(name, shape, dt, offset=MAPBASE + off)

        id_sb = at("id_sb", [128, 128], BF16, 0)
        onescol = at("onescol", [128, 1], BF16, 256)
        onesrow = at("onesrow", [1, 128], BF16, 288)
        mexp_sb = at("mexp_sb", [128, 1], F32, 544)
        exl_sb = at("exl_sb", [16, NDC * 128], BF16, 576)
        exr_sb = at("exr_sb", [8, 4 * 128], BF16, 2624)
        gbd_sb = at("gbd_sb", [128, NDC * 32], BF16, 3648)
        gw_sb = at("gw_sb", [128, NDC * 8], BF16, 4160)
        b1_sb = at("b1_sb", [128, NHC], F32, 4288)
        b2_sb = at("b2_sb", [1, D], BF16, 4416)
        gb_sb = at("gb_sb", [1, 8], BF16, 6464)
        onehot_sb = at("onehot_sb", [128, 8], F32, 6496)
        gv2_sb = at("gv2_sb", [128, LD], BF16, 6528)
        gkt_sb = at("gkt_sb", [128, 4 * 64], BF16, 7552)
        pooledT = at("pooledT", [128, NDC * 64], BF16, 8064)
        pool_stage = at("pool_stage", [1, D], F32, 202944)
        wq_sb = at("wq_sb", [128, NDC * D], BF16, 10240)
        wk_sb = at("wk_sb", [128, NDC * D], BF16, 26624)
        wv_sb = at("wv_sb", [128, NDC * D], BF16, 43008)
        wout_sb = at("wout_sb", [128, NDC * D], BF16, 59392)
        wgq_sb = at("wgq_sb", [128, NDC * LD], BF16, 75776)
        wgk_sb = at("wgk_sb", [128, NDC * LD], BF16, 83968)
        wgv_sb = at("wgv_sb", [128, NDC * LD], BF16, 92160)
        rout_sb = at("rout_sb", [128, 4 * D], BF16, 100352)
        kt_roll = at("kt_roll", [128, 2 * 2048], BF16, 108544)
        v_roll = at("v_roll", [128, 2 * 2048], BF16, 116736)
        xlnT_buf = at("xlnT_buf", [128, 2 * 2048], BF16, 124928)
        qt_qg = at("qt_qg", [128, NDC * 256], BF16, 133120)
        gqt_qg = at("gqt_qg", [128, 4 * 256], BF16, 137216)
        p_wave = at("p_wave", [128, 2 * 8 * 256], BF16, 139264)
        mask_sb = at("mask_sb", [128, 2 * 4 * 256], BF16, 147456)
        latmask_sb = at("latmask_sb", [128, 2 * 256], BF16, 151552)
        p_lat = at("p_lat", [128, 4 * 256], BF16, 152576)
        local_sb = at("local_sb", [128, 8 * 256], BF16, 154624)
        stage_sums = at("stage_sums", [128, 8 * 256], F32, 158720)
        stage_lat = at("stage_lat", [128, 4 * 256], F32, 166912)
        recipL_raw = at("recipL_raw", [16, 256], F32, 171008)
        recipR_raw = at("recipR_raw", [8, 256], F32, 172032)
        recipR_bf = at("recipR_bf", [8, 256], BF16, 173056)
        gates_bf = at("gates_bf", [16, 512], BF16, 173568)
        glr_bf = at("glr_bf", [16, 256], BF16, 174592)
        xbuf = at("xbuf", [128, 2 * D], F32, 175104)
        xlnbuf = at("xlnbuf", [128, 2 * D], BF16, 183296)
        xlnT_stage = at("xlnT_stage", [128, 2 * D], BF16, 187392)
        sqdump = at("sqdump", [128, D], F32, 191488)
        negsum = at("negsum", [128, NT], F32, 195584)
        sumsq = at("sumsq", [128, NT], F32, 195712)
        negmean = at("negmean", [128, NT], F32, 195840)
        msq = at("msq", [128, NT], F32, 195968)
        vareps = at("vareps", [128, NT], F32, 196096)
        lnv = at("lnv", [128, NT], F32, 196224)
        rstd = at("rstd", [128, NT], F32, 196352)
        pooled_sb = at("pooled_sb", [64, D], F32, 196480)
        pooled_bf = at("pooled_bf", [64, D], BF16, 200576)
        e8buf = at("e8buf", [128, 16], F32, 202624)
        m_sb = at("m_sb", [128, 2 * 8], F32, 202688)
        srt8 = at("srt8", [128, 8], F32, 202752)
        sel8 = at("sel8", [128, 8], F32, 202784)
        negmax8 = at("negmax8", [128, 2], F32, 202816)
        s8c = at("s8c", [128, 2], F32, 202848)
        r8c = at("r8c", [128, 2], F32, 202880)
        g_loc = at("g_loc", [128, 8 * 256], BF16, 83968)
        g_gr = at("g_gr", [128, 8 * 256], BF16, 88064)
        g_rn = at("g_rn", [128, 4 * 256], BF16, 92160)
        rem_norm = at("rem_norm", [128, 4 * 256], BF16, 94208)
        mix_sb = at("mix_sb", [128, 8 * 256], BF16, 96256)
        tmp1 = stage_sums.ap()
        tmp2 = p_wave.ap().bitcast(F32)
        w1sb = at("w1sb", [128, NDC * HID], BF16, 10240)
        w2sb = at("w2sb", [128, NHC * D], BF16, 75776)
        xsl = at("xsl", [128, 2 * NDC * KSLICE], BF16, 141312)
        hid = at("hid", [128, NHC * KSLICE], BF16, 157696)
        outsb = at("outsb", [128, 2 * D], F32, 190464)
        mt_sb = at("mt_sb", [128, 2 * 4 * 8], F32, 198656)
        mcol = at("mcol", [128, 2 * 4], F32, 198912)
        mdump = at("mdump", [128, 64], F32, 202944)
        fxr_sb = at("fxr_sb", [128, D], F32, 203200)
        ffs_sb = at("ffs_sb", [128, D], F32, 207296)

        psA = es.enter_context(nc.psum_tensor("psA", [128, 1024], F32))
        psB = es.enter_context(nc.psum_tensor("psB", [128, 1024], F32))
        psC = es.enter_context(nc.psum_tensor("psC", [128, 1024], F32))
        psD = es.enter_context(nc.psum_tensor("psD", [128, 1024], F32))
        psA_bf = psA.ap().bitcast(BF16)
        psB_bf = psB.ap().bitcast(BF16)

        names = ("s_pvev s_pvea s_lpd s_gv2d s_w0 s_w1 s_w2 s_w3 s_w4 s_w5 "
                 "s_w6 s_w7 s_w8 s_wc s_sd0 s_sd1 s_sd2 s_sd3 s_px0 s_px1 "
                 "s_xlnt s_mask s_latmask s_xb s_xtd0 s_xtd1 s_st1 s_sq s_vv "
                 "s_ss s_ln s_tp s_tpe s_poolev s_poold s_cc s_lp s_prj "
                 "s_prje s_sc s_pexp s_p s_pv s_gt s_gte s_lsc s_lexp s_lp2 "
                 "s_lpv s_lpve s_rcp s_ex s_exe s_rn s_ro s_mx s_wo s_rese "
                 "s_out s_ln2 s_tp2 s_tpe2 s_x2d0 s_x2d1 s_lgm s_lge s_md0 "
                 "s_md1 s_f1a s_f1b s_f2a s_f2b s_f2c s_f2d s_fxs0 s_fxs1 "
                 "s_hmm s_hev_e s_hev_o s_omm s_oev s_od0 s_od1 s_y0 s_y1 s_fl0 "
                 "s_fl1 s_fadd s_out1").split()
        SM = {n: es.enter_context(nc.semaphore(n)) for n in names}

        FIN_AT = {t: 8 * (t // 4) + 12 + (t % 4) for t in range(NOT)}

        block = es.enter_context(nc.Block())

        def prj_slot(j):
            s = j % 4
            t_ = psA if s < 2 else psB
            return t_[:, (s % 2) * 512:(s % 2) * 512 + 512]

        def sc_slot(tau):
            s = tau % 8
            t_ = psA if (s % 4) < 2 else psB
            col = (s % 2) * 512 + (s // 4) * 256
            return t_[:, col:col + 256]

        def ex_slot(j):
            s = j % 8
            t_ = psB if s < 4 else psC
            return t_[:, (s % 4) * 256:(s % 4) * 256 + 256]

        EARLY_STATE = [None]

        @block.gpsimd
        def _(g):
            wspecs = [(wq_sb, wq, D), (wk_sb, wk, D), (wv_sb, wv, D),
                      (wout_sb, woutd, D), (wgq_sb, wgq, LD), (wgk_sb, wgk, LD),
                      (wgv_sb, wgv, LD), (gbd_sb, gbdd, 32)]
            for ci in range(NDC):
                for wi, (w_sb_, wd, ow) in enumerate(wspecs):
                    if ci >= 1:
                        g.wait_ge(SM[f"s_w{wi}"], ci * 16)
                    g.dma_start(out=w_sb_[:, ci * ow:(ci + 1) * ow],
                                in_=wd[ci * 128:(ci + 1) * 128, :]
                                ).then_inc(SM[f"s_w{wi}"], 16)
            cnt8 = 0
            for ci in range(4):
                if cnt8 >= 1:
                    g.wait_ge(SM["s_w8"], cnt8 * 16)
                g.dma_start(out=rout_sb[:, ci * D:(ci + 1) * D],
                            in_=routd[ci * 128:(ci + 1) * 128, :]
                            ).then_inc(SM["s_w8"], 16)
                cnt8 += 1
            for dst_, src_ in ((b2_sb[:], b2d[:]), (gb_sb[:], gbias[:])):
                g.wait_ge(SM["s_w8"], cnt8 * 16)
                g.dma_start(out=dst_, in_=src_).then_inc(SM["s_w8"], 16)
                cnt8 += 1
            for ci in range(NDC):
                g.wait_ge(SM["s_w8"], cnt8 * 16)
                g.dma_start(out=gw_sb[:, ci * 8:(ci + 1) * 8],
                            in_=gwd[ci * 128:(ci + 1) * 128, :]
                            ).then_inc(SM["s_w8"], 16)
                cnt8 += 1
            for n in range(8):
                g.wait_ge(SM["s_poolev"], n + 1)
                g.dma_start(out=cc_pool_in.ap()[n:n + 1, :],
                            in_=pool_stage[:]).then_inc(SM["s_poold"], 16)
            g.wait_ge(SM["s_poold"], 8 * 16)
            g.collective_compute(
                "AllGather", ALU.bypass, replica_groups=[list(range(NCORES))],
                ins=[cc_pool_in.ap()], outs=[cc_pool_out.ap()]).then_inc(SM["s_cc"], 1)
            g.wait_ge(SM["s_cc"], 1)
            g.dma_start(out=pooled_sb[:], in_=cc_pool_out.ap()[:]
                        ).then_inc(SM["s_lpd"], 16)
            for gq in range(NQG):
                if gq == 4:
                    g.wait_ge(SM["s_x2d0"], 4 * 16)
                    g.wait_ge(SM["s_x2d1"], 4 * 16)
                    g.collective_compute(
                        "AllGather", ALU.bypass,
                        replica_groups=[list(range(NCORES))],
                        ins=[cc_x_inh[0].ap()],
                        outs=[cc_x_outh[0].ap()]).then_inc(SM["s_cc"], 1)
                    g.wait_ge(SM["s_md0"], 4 * 16)
                    g.wait_ge(SM["s_md1"], 4 * 16)
                    g.collective_compute(
                        "AllGather", ALU.bypass,
                        replica_groups=[list(range(NCORES))],
                        ins=[cc_m_in.ap()[0:T // 2, :].opt()],
                        outs=[cc_m_outh[0].ap()]).then_inc(SM["s_cc"], 1)
                g.wait_ge(SM["s_pvev"], (gq + 1) * 8)
                g.wait_ge(SM["s_pvea"], (gq + 1) * 16)
                g.dma_start(out=recipL_raw.ap()[0:8, :],
                            in_=stage_sums[0:1, :]).then_inc(SM["s_sd0"], 16)
                g.dma_start(out=recipL_raw.ap()[8:16, :],
                            in_=stage_sums[64:65, :]).then_inc(SM["s_sd1"], 16)
                g.wait_ge(SM["s_lpve"], (gq + 1) * 8)
                g.dma_start(out=recipR_raw.ap()[0:4, :],
                            in_=stage_lat[0:1, :]).then_inc(SM["s_sd2"], 16)
                g.dma_start(out=recipR_raw.ap()[4:8, :],
                            in_=stage_lat[64:65, :]).then_inc(SM["s_sd3"], 16)
                if gq == 7:
                    g.wait_ge(SM["s_prj"], PRJ_TOT[NQG])
                    f1 = [0, 0]
                    for d_ in range(6):
                        sem = "s_f1a" if d_ % 2 == 0 else "s_f1b"
                        i2 = d_ % 2
                        if f1[i2] >= 1:
                            g.wait_ge(SM[sem], f1[i2] * 16)
                        g.dma_start(out=w1sb[:, d_ * HID:(d_ + 1) * HID],
                                    in_=w1d[d_ * 128:(d_ + 1) * 128, :]
                                    ).then_inc(SM[sem], 16)
                        f1[i2] += 1
                    f2 = [0, 0, 0, 0]
                    for h_ in range(4):
                        i4 = h_ % 4
                        sem = ["s_f2a", "s_f2b", "s_f2c", "s_f2d"][i4]
                        g.dma_start(out=w2sb[:, h_ * D:(h_ + 1) * D],
                                    in_=w2d[h_ * 128:(h_ + 1) * 128, :]
                                    ).then_inc(SM[sem], 16)
                        f2[i4] += 1
                    EARLY_STATE[0] = (f1, f2)
            g.wait_ge(SM["s_out"], NQG * 16)
            g.wait_ge(SM["s_out1"], NQG * 16)
            g.wait_ge(SM["s_wo"], 4 * NQG)
            g.wait_ge(SM["s_mx"], 8 * NQG)
            g.wait_ge(SM["s_ro"], 8 * NQG)
            f1, f2 = EARLY_STATE[0]
            for d_ in range(6, NDC):
                sem = "s_f1a" if d_ % 2 == 0 else "s_f1b"
                i2 = d_ % 2
                if f1[i2] >= 1:
                    g.wait_ge(SM[sem], f1[i2] * 16)
                g.dma_start(out=w1sb[:, d_ * HID:(d_ + 1) * HID],
                            in_=w1d[d_ * 128:(d_ + 1) * 128, :]
                            ).then_inc(SM[sem], 16)
                f1[i2] += 1
            for h_ in range(4, NHC):
                i4 = h_ % 4
                sem = ["s_f2a", "s_f2b", "s_f2c", "s_f2d"][i4]
                if f2[i4] >= 1:
                    g.wait_ge(SM[sem], f2[i4] * 16)
                g.dma_start(out=w2sb[:, h_ * D:(h_ + 1) * D],
                            in_=w2d[h_ * 128:(h_ + 1) * 128, :]
                            ).then_inc(SM[sem], 16)
                f2[i4] += 1
            g.wait_ge(SM["s_x2d0"], 8 * 16)
            g.wait_ge(SM["s_x2d1"], 8 * 16)
            g.collective_compute(
                "AllGather", ALU.bypass,
                replica_groups=[list(range(NCORES))],
                ins=[cc_x_inh[1].ap()],
                outs=[cc_x_outh[1].ap()]).then_inc(SM["s_cc"], 1)
            g.wait_ge(SM["s_md0"], 8 * 16)
            g.wait_ge(SM["s_md1"], 8 * 16)
            g.collective_compute(
                "AllGather", ALU.bypass,
                replica_groups=[list(range(NCORES))],
                ins=[cc_m_in.ap()[T // 2:T, :].opt()],
                outs=[cc_m_outh[1].ap()]).then_inc(SM["s_cc"], 1)
            for k8 in range(2 * NK):
                g.wait_ge(SM["s_od0" if k8 % 2 == 0 else "s_od1"],
                          (8 * (k8 // 2) + 8) * 2 * 16)
                g.collective_compute(
                    "ReduceScatter", ALU.add,
                    replica_groups=[list(range(NCORES))],
                    ins=[rs_in[k8].ap()], outs=[rs_out[k8].ap()]).then_inc(SM["s_cc"], 1)


        @block.sync
        def _(sy):
            sy.dma_start(out=id_sb[:], in_=identd[:]).then_inc(SM["s_wc"], 16)
            sy.dma_start(out=exl_sb[:], in_=exld[:]).then_inc(SM["s_wc"], 16)
            sy.dma_start(out=exr_sb[:], in_=exrd[:]).then_inc(SM["s_wc"], 16)
            sy.dma_start(out=b1_sb[:], in_=b1d[:]).then_inc(SM["s_wc"], 16)
            sy.dma_start(out=onehot_sb[:], in_=onehotd[:]).then_inc(SM["s_wc"], 16)
            for t in range(NT):
                if t >= 2:
                    sy.wait_ge(SM["s_st1"], t - 1)
                    sy.wait_ge(SM["s_sq"], t - 1)
                sy.dma_start(out=xbuf[:, (t % 2) * D:(t % 2 + 1) * D],
                             in_=xdr[t * 128:(t + 1) * 128, :]
                             ).then_inc(SM["s_px0" if t % 2 == 0 else "s_px1"], 16)
            for t in range(NT):
                if t < 2:
                    sy.wait_ge(SM["s_st1"], NT - 1 + t)
                    sy.wait_ge(SM["s_sq"], NT - 1 + t)
                else:
                    sy.wait_ge(SM["s_ln"], t - 1)
                sy.dma_start(out=xbuf[:, (t % 2) * D:(t % 2 + 1) * D],
                             in_=xdr[t * 128:(t + 1) * 128, :]
                             ).then_inc(SM["s_px0" if t % 2 == 0 else "s_px1"], 16)
            xlcnt = [0]
            for gq in range(NQG):
                gb = gq % 2
                if gq >= 1:
                    sy.wait_ge(SM["s_prj"], PRJ_TOT[gq])
                if gq >= 2:
                    sy.wait_ge(SM["s_p"], 64 * (gq - 1))
                    sy.wait_ge(SM["s_lp2"], 4 * (gq - 1))
                for p in pairs_of(gq):
                    pb = p % 2
                    for sub in range(2):
                        t = 2 * p + sub
                        xlcnt[0] += 1
                        if xlcnt[0] > 1:
                            sy.wait_ge(SM["s_xlnt"], (xlcnt[0] - 1) * 16)
                        sy.wait_ge(SM["s_xtd0" if t % 2 == 0 else "s_xtd1"],
                                   (t // 2 + 1) * 16)
                        dst = xlnT_buf.ap()[:, pb * 2048:(pb + 1) * 2048] \
                            .rearrange("a (c u) -> a c u", c=8)[:, :, sub * 128:(sub + 1) * 128]
                        src = xlnT_dram.ap()[t].transpose([1, 0, 2])
                        sy.dma_start(out=dst, in_=src).then_inc(SM["s_xlnt"], 16)
                if gq >= 1:
                    sy.wait_ge(SM["s_mask"], gq * 16)
                sy.dma_start(out=mask_sb[:, gb * 1024:(gb + 1) * 1024],
                             in_=masksd[gq].transpose([1, 0, 2])
                             ).then_inc(SM["s_mask"], 16)
                if gq >= 1:
                    sy.wait_ge(SM["s_latmask"], gq * 16)
                sy.dma_start(out=latmask_sb[:, gb * 256:(gb + 1) * 256],
                             in_=latmaskd[gq]).then_inc(SM["s_latmask"], 16)
                if gq >= 1:
                    sy.wait_ge(SM["s_out"], 16 * gq)
                    sy.wait_ge(SM["s_out1"], 16 * gq)
                    sy.wait_ge(SM["s_ln2"], 7 * gq - 4)
                    sy.wait_ge(SM["s_sq"], NT + 2 * gq)
                for sub in range(2):
                    t = 2 * (gq + 1) + sub
                    if gq == 0 and sub == 0:
                        sy.wait_ge(SM["s_ln"], NT)
                    if gq > 0 or sub > 0:
                        sy.wait_ge(SM["s_xb"], (2 * gq + sub) * 16)
                    sy.dma_start(out=xbuf[:, sub * D:(sub + 1) * D],
                                 in_=xdr[t * 128:(t + 1) * 128, :]
                                 ).then_inc(SM["s_xb"], 16)
                if gq == 0:
                    sy.wait_ge(SM["s_lp"], 7)
                    sy.dma_start(out=gv2_sb[64:128, :], in_=gv2_sb[0:64, :]
                                 ).then_inc(SM["s_gv2d"], 16)
            sy.wait_ge(SM["s_p"], 64 * NQG)
            sy.wait_ge(SM["s_lp2"], 4 * NQG)
            sy.wait_ge(SM["s_lpv"], 8 * NQG)
            sy.wait_ge(SM["s_mx"], 8 * NQG)
            def emit_final_load(t):
                k = t // 4
                sy.wait_ge(SM["s_cc"], 6 + 2 * k + (t % 4) // 2)
                if t >= 1:
                    sy.wait_ge(SM["s_y0" if (t - 1) % 2 == 0 else "s_y1"],
                               ((t - 1) // 2 + 1) * 16)
                    sy.wait_ge(SM["s_fadd"], t)
                sy.wait_ge(SM["s_fl0"], 2 * t * 16)
                sy.dma_start(out=fxr_sb[:, 0:D],
                             in_=xres_dram.ap()[t * 128:(t + 1) * 128, :]
                             ).then_inc(SM["s_fl0"], 16)
                sy.wait_ge(SM["s_fl0"], (2 * t + 1) * 16)
                sy.dma_start(out=ffs_sb[:, 0:D],
                             in_=rs_out[2 * k + (t % 4) // 2].ap()[
                                 ((t % 4) % 2) * 128:((t % 4) % 2 + 1) * 128, :]
                             ).then_inc(SM["s_fl0"], 16)

            for i in range(NSL):
                k, r = divmod(i, NCORES)
                sy.wait_ge(SM["s_cc"], 3 if k < 2 else 5)
                buf = i % 2
                psem = "s_fxs0" if buf == 0 else "s_fxs1"
                if i >= 2:
                    sy.wait_ge(SM["s_hmm"], (i - 1) * NHC)
                fcnt = (i // 2) * 12
                h_ = k // 2
                kk = k % 2
                for d_ in range(NDC):
                    if d_ % 2 == 0 and fcnt > 0:
                        sy.wait_ge(SM[psem], fcnt * 16)
                    sy.dma_start(
                        out=xsl[:, (buf * NDC + d_) * KSLICE:(buf * NDC + d_ + 1) * KSLICE],
                        in_=cc_x_outh[h_].ap()[r, d_ * 128:(d_ + 1) * 128,
                                               kk * KSLICE:(kk + 1) * KSLICE],
                    ).then_inc(SM[psem], 16)
                    fcnt += 1
                for qt in range(4):
                    sy.wait_ge(SM[psem], fcnt * 16)
                    sy.dma_start(
                        out=mt_sb[:, (buf * 4 + qt) * 8:(buf * 4 + qt + 1) * 8],
                        in_=cc_m_outh[h_].ap()[r, kk * KSLICE + qt * 128:
                                               kk * KSLICE + (qt + 1) * 128, :],
                    ).then_inc(SM[psem], 16)
                    fcnt += 1
                for t in range(NOT):
                    if FIN_AT.get(t) == i:
                        emit_final_load(t)
            for t in range(NOT):
                if FIN_AT.get(t, 99) > NSL - 1:
                    emit_final_load(t)

        @block.vector
        def _(v):
            vv = [1]

            def vsync(ins):
                ins.then_inc(SM["s_vv"], 1)
                vv[0] += 1
                v.wait_ge(SM["s_vv"], vv[0])

            v.memset(onescol[:], 1.0)
            v.memset(onesrow[:], 1.0)
            v.memset(mexp_sb[:], -M_EXP).then_inc(SM["s_vv"], 1)
            for t in range(NT):
                v.wait_ge(SM["s_px0" if t % 2 == 0 else "s_px1"], (t // 2 + 1) * 16)
                v.reduce_sum(negsum[:, t:t + 1],
                             xbuf[:, (t % 2) * D:(t % 2 + 1) * D],
                             axis=AXX, negate=True).then_inc(SM["s_st1"], 1)
            v.wait_ge(SM["s_sq"], NT)
            v.wait_ge(SM["s_st1"], NT)
            vsync(v.tensor_scalar(negmean[:, 0:NT], negsum[:, 0:NT], 1.0 / D,
                                  None, op0=ALU.mult))
            vsync(v.tensor_mul(msq[:, 0:NT], negmean[:, 0:NT], negmean[:, 0:NT]))
            vsync(v.scalar_tensor_tensor(vareps[:, 0:NT], sumsq[:, 0:NT], 1.0 / D,
                                         msq[:, 0:NT], op0=ALU.mult,
                                         op1=ALU.subtract))
            v.tensor_scalar(vareps[:, 0:NT], vareps[:, 0:NT], EPS, None,
                            op0=ALU.add).then_inc(SM["s_vv"], 1)
            vv[0] += 1
            for t in range(NT):
                v.wait_ge(SM["s_px0" if t % 2 == 0 else "s_px1"],
                          (NT // 2 + t // 2 + 1) * 16)
                if t == 0:
                    v.wait_ge(SM["s_ss"], 2)
                if t >= 2:
                    v.wait_ge(SM["s_tp"], t - 1)
                v.tensor_scalar(xlnbuf[:, (t % 2) * D:(t % 2 + 1) * D],
                                xbuf[:, (t % 2) * D:(t % 2 + 1) * D],
                                negmean[:, t:t + 1], rstd[:, t:t + 1],
                                op0=ALU.add, op1=ALU.mult).then_inc(SM["s_ln"], 1)
                v.wait_ge(SM["s_tp"], t + 1)
                src = psA_bf if t % 2 == 0 else psB_bf
                if t >= 2:
                    v.wait_ge(SM["s_xtd0" if t % 2 == 0 else "s_xtd1"],
                              (t // 2) * 16)
                v.tensor_copy(xlnT_stage[:, (t % 2) * D:(t % 2 + 1) * D],
                              src[:, 0:1024]).then_inc(SM["s_tpe"], 1)
            v.wait_ge(SM["s_lpd"], 16)
            v.tensor_copy(pooled_bf[:], pooled_sb[:]).then_inc(SM["s_lp"], 1)
            v.wait_ge(SM["s_lp"], 2)
            v.tensor_copy(pooledT[:], psA_bf[:, 0:NDC * 64]).then_inc(SM["s_lp"], 1)
            v.wait_ge(SM["s_lp"], 4)
            v.tensor_copy(gkt_sb[:], psB[:, 0:256]).then_inc(SM["s_lp"], 1)
            v.wait_ge(SM["s_lp"], 6)
            v.tensor_copy(gv2_sb[0:64, :], psC[0:64, 0:512]).then_inc(SM["s_lp"], 1)
            def emit_top2(gq2):
                    for sub in range(2):
                        t2 = 2 * gq2 + sub
                        if gq2 == 0:
                            v.wait_ge(SM["s_xtd0" if sub == 0 else "s_xtd1"],
                                      (NT // 2) * 16)
                        v.wait_ge(SM["s_tp2"], t2 + 1)
                        srcb = psA_bf if sub == 0 else psB_bf
                        if t2 >= 2:
                            v.wait_ge(SM["s_x2d0" if t2 % 2 == 0 else "s_x2d1"],
                                      (t2 // 2) * 16)
                            v.wait_ge(SM["s_lgm"], t2 - 1)
                        v.tensor_copy(xlnT_stage[:, sub * D:(sub + 1) * D],
                                      srcb[:, 0:1024]).then_inc(SM["s_tpe2"], 1)
                    for sub in range(2):
                        t2 = 2 * gq2 + sub
                        v.wait_ge(SM["s_lgm"], t2 + 1)
                        v.reduce_max(negmax8[:, sub:sub + 1],
                                         psD[:, 512 + sub * 8:520 + sub * 8],
                                         axis=AXX, negate=True).then_inc(SM["s_ln2"], 1)
                        v.wait_ge(SM["s_lge"], t2 + 1)
                        vsync(v.reduce_sum(s8c[:, sub:sub + 1],
                                               e8buf[:, sub * 8:(sub + 1) * 8], axis=AXX))
                        vsync(v.reciprocal(r8c[:, sub:sub + 1], s8c[:, sub:sub + 1]))
                        vsync(v.tensor_scalar(e8buf[:, sub * 8:(sub + 1) * 8],
                                                  e8buf[:, sub * 8:(sub + 1) * 8],
                                                  r8c[:, sub:sub + 1], None,
                                                  op0=ALU.mult))
                        vsync(v.max(out=srt8[:], in_=e8buf[:, sub * 8:(sub + 1) * 8]))
                        vsync(v.tensor_scalar(sel8[:], e8buf[:, sub * 8:(sub + 1) * 8],
                                                  srt8[:, 1:2], None, op0=ALU.is_ge))
                        if t2 >= 2:
                            v.wait_ge(SM["s_md0" if t2 % 2 == 0 else "s_md1"],
                                          (t2 // 2) * 16)
                        v.tensor_mul(m_sb[:, sub * 8:(sub + 1) * 8],
                                         e8buf[:, sub * 8:(sub + 1) * 8], sel8[:]
                                         ).then_inc(SM["s_ln2"], 1)
            for gq in range(NQG):
                gb = gq % 2
                j = PRJ_TOT[gq]
                for p in pairs_of(gq):
                    sp = p % 2
                    for co in range(NDC):
                        v.wait_ge(SM["s_prj"], j + 1)
                        v.tensor_copy(
                            kt_roll[:, sp * 2048 + co * 256:sp * 2048 + (co + 1) * 256],
                            prj_slot(j)[:, 0:256]).then_inc(SM["s_prje"], 1)
                        j += 1
                    for sub in range(2):
                        for dhf in range(2):
                            v.wait_ge(SM["s_prj"], j + 1)
                            v.tensor_copy(
                                v_roll[:, sp * 2048 + sub * 1024 + dhf * 512:
                                       sp * 2048 + sub * 1024 + (dhf + 1) * 512],
                                prj_slot(j)).then_inc(SM["s_prje"], 1)
                            j += 1
                for co in range(NDC):
                    v.wait_ge(SM["s_prj"], j + 1)
                    v.tensor_copy(qt_qg[:, co * 256:(co + 1) * 256],
                                  prj_slot(j)[:, 0:256]).then_inc(SM["s_prje"], 1)
                    j += 1
                for co in range(4):
                    v.wait_ge(SM["s_prj"], j + 1)
                    v.tensor_copy(gqt_qg[:, co * 256:(co + 1) * 256],
                                  prj_slot(j)[:, 0:256]).then_inc(SM["s_prje"], 1)
                    j += 1
                if gq >= 1:
                    emit_top2(gq - 1)
                v.wait_ge(SM["s_mask"], (gq + 1) * 16)
                for w in range(8):
                    wp = w % 2
                    for jt in range(8):
                        tau = 64 * gq + w * 8 + jt
                        kc = jt % 4
                        v.wait_ge(SM["s_pexp"], tau + 1)
                        v.tensor_mul(
                            p_wave[:, (wp * 8 + jt) * 256:(wp * 8 + jt + 1) * 256],
                            p_wave[:, (wp * 8 + jt) * 256:(wp * 8 + jt + 1) * 256],
                            mask_sb[:, gb * 1024 + kc * 256:gb * 1024 + (kc + 1) * 256],
                        ).then_inc(SM["s_p"], 1)
                    v.wait_ge(SM["s_pv"], 8 * gq + w + 1)
                    v.tensor_copy(local_sb[:, w * 256:(w + 1) * 256],
                                  psC[:, (w % 2) * 512:(w % 2) * 512 + 256]
                                  ).then_inc(SM["s_pvev"], 1)
                v.wait_ge(SM["s_latmask"], (gq + 1) * 16)
                for pt in range(4):
                    v.wait_ge(SM["s_lexp"], 4 * gq + pt + 1)
                    v.tensor_mul(p_lat[:, pt * 256:(pt + 1) * 256],
                                 p_lat[:, pt * 256:(pt + 1) * 256],
                                 latmask_sb[:, gb * 256:(gb + 1) * 256]
                                 ).then_inc(SM["s_lp2"], 1)
                v.wait_ge(SM["s_sd0"], (gq + 1) * 16)
                v.wait_ge(SM["s_sd1"], (gq + 1) * 16)
                vsync(v.reciprocal(recipL_raw[:], recipL_raw[:]))
                v.wait_ge(SM["s_gte"], gq + 1)
                v.tensor_mul(glr_bf[:], gates_bf[:, 0:256], recipL_raw[:]
                             ).then_inc(SM["s_rcp"], 1)
                v.wait_ge(SM["s_sd2"], (gq + 1) * 16)
                v.wait_ge(SM["s_sd3"], (gq + 1) * 16)
                vsync(v.reciprocal(recipR_raw[:], recipR_raw[:]))
                v.tensor_copy(recipR_bf[:], recipR_raw[:]).then_inc(SM["s_rcp"], 1)
                v.wait_ge(SM["s_exe"], 20 * (gq + 1))
                v.wait_ge(SM["s_lpv"], 8 * (gq + 1))
                for c in range(4):
                    v.tensor_mul(rem_norm[:, c * 256:(c + 1) * 256],
                                 psA[:, c * 256:(c + 1) * 256],
                                 g_rn[:, c * 256:(c + 1) * 256]
                                 ).then_inc(SM["s_rn"], 1)
                last = None
                for c in range(8):
                    last = v.tensor_mul(tmp2[:, c * 256:(c + 1) * 256],
                                        local_sb[:, c * 256:(c + 1) * 256],
                                        g_loc[:, c * 256:(c + 1) * 256])
                vsync(last)
                for half in range(2):
                    last = None
                    for c in range(4 * half, 4 * half + 4):
                        v.wait_ge(SM["s_ro"], 8 * gq + (c // 2) * 2 + 2)
                        srcp = psC if c < 4 else psD
                        last = v.tensor_mul(tmp1[:, c * 256:(c + 1) * 256],
                                            srcp[:, (c % 4) * 256:(c % 4 + 1) * 256],
                                            g_gr[:, c * 256:(c + 1) * 256])
                    vsync(last)
                    for c in range(4 * half, 4 * half + 4):
                        v.tensor_add(mix_sb[:, c * 256:(c + 1) * 256],
                                     tmp1[:, c * 256:(c + 1) * 256],
                                     tmp2[:, c * 256:(c + 1) * 256]
                                     ).then_inc(SM["s_mx"], 1)
                v.wait_ge(SM["s_wo"], 4 * (gq + 1))
                v.wait_ge(SM["s_out"], 16 * gq)
                v.wait_ge(SM["s_out1"], 16 * gq)
                v.wait_ge(SM["s_xb"], (2 * gq + 2) * 16)
                for sub in range(2):
                    srcp = psA if sub == 0 else psB
                    for dhf in range(2):
                        ins_ = v.tensor_add(
                            xbuf[:, sub * D + dhf * 512:sub * D + (dhf + 1) * 512],
                            xbuf[:, sub * D + dhf * 512:sub * D + (dhf + 1) * 512],
                            srcp[:, dhf * 512:(dhf + 1) * 512])
                    ins_.then_inc(SM["s_rese"], 1)
                v.wait_ge(SM["s_rese"], 2 * (gq + 1))
                last = None
                for sub in range(2):
                    t2 = 2 * gq + sub
                    last = v.reduce_sum(negsum[:, t2:t2 + 1],
                                        xbuf[:, sub * D:(sub + 1) * D],
                                        axis=AXX, negate=True)
                vsync(last)
                v.wait_ge(SM["s_sq"], NT + 2 * (gq + 1))
                sl2 = slice(2 * gq, 2 * gq + 2)
                vsync(v.tensor_scalar(negmean[:, sl2], negsum[:, sl2], 1.0 / D,
                                      None, op0=ALU.mult))
                vsync(v.tensor_mul(msq[:, sl2], negmean[:, sl2], negmean[:, sl2]))
                vsync(v.scalar_tensor_tensor(vareps[:, sl2], sumsq[:, sl2],
                                             1.0 / D, msq[:, sl2],
                                             op0=ALU.mult, op1=ALU.subtract))
                v.tensor_scalar(vareps[:, sl2], vareps[:, sl2], EPS, None,
                                op0=ALU.add).then_inc(SM["s_ln2"], 1)
                v.wait_ge(SM["s_ss"], 2 + 2 * (gq + 1))
                for sub in range(2):
                    t2 = 2 * gq + sub
                    v.tensor_scalar(xlnbuf[:, sub * D:(sub + 1) * D],
                                    xbuf[:, sub * D:(sub + 1) * D],
                                    negmean[:, t2:t2 + 1], rstd[:, t2:t2 + 1],
                                    op0=ALU.add, op1=ALU.mult).then_inc(SM["s_ln2"], 1)
            emit_top2(NQG - 1)
            v.wait_ge(SM["s_out"], NQG * 16)
            v.wait_ge(SM["s_out1"], NQG * 16)
            v.wait_ge(SM["s_x2d0"], 8 * 16)
            v.wait_ge(SM["s_x2d1"], 8 * 16)
            for sdn in ("s_sd0", "s_sd1", "s_sd2", "s_sd3"):
                v.wait_ge(SM[sdn], NQG * 16)
            def emit_final_add(t):
                v.wait_ge(SM["s_fl0"], (2 * t + 2) * 16)
                v.tensor_add(fxr_sb[:, 0:D], fxr_sb[:, 0:D],
                             ffs_sb[:, 0:D]).then_inc(SM["s_fadd"], 1)

            for i in range(NSL):
                k, r = divmod(i, NCORES)
                buf = i % 2
                psem = "s_fxs0" if buf == 0 else "s_fxs1"
                v.wait_ge(SM[psem], (i // 2 + 1) * 12 * 16)
                last = None
                for qt in range(4):
                    last = v.scalar_tensor_tensor(
                        mdump[:, (buf * 4 + qt) * 8:(buf * 4 + qt + 1) * 8],
                        mt_sb[:, (buf * 4 + qt) * 8:(buf * 4 + qt + 1) * 8],
                        1.0, onehot_sb[:], op0=ALU.mult, op1=ALU.mult,
                        accum_out=mcol[:, buf * 4 + qt:buf * 4 + qt + 1])
                vsync(last)
                for hc in range(1, NHC, 2):
                    v.wait_ge(SM["s_hmm"], i * NHC + hc + 1)
                    v.tensor_scalar(
                        hid[:, hc * KSLICE:(hc + 1) * KSLICE],
                        psA[:, 512:1024],
                        b1_sb[:, hc:hc + 1], 0.0,
                        op0=ALU.add, op1=ALU.max).then_inc(SM["s_hev_o"], 1)
                for g_ in range(4):
                    idx = i * 4 + g_
                    v.wait_ge(SM["s_omm"], idx * 2 + 2)
                    if idx >= 2:
                        pidx = idx - 2
                        v.wait_ge(SM["s_od0" if (pidx % 4) < 2 else "s_od1"],
                                  ((pidx // 4) * 2 + 2) * 16)
                    ob = idx % 2
                    for dhf in range(2):
                        bank = (g_ % 2) * 2 + dhf
                        srcp = psC if bank < 2 else psD
                        v.tensor_scalar(
                            outsb[:, ob * D + dhf * 512:ob * D + (dhf + 1) * 512],
                            srcp[:, (bank % 2) * 512:(bank % 2 + 1) * 512],
                            mcol[:, buf * 4 + g_:buf * 4 + g_ + 1], None,
                            op0=ALU.mult).then_inc(SM["s_oev"], 1)
                for t in range(NOT):
                    if FIN_AT.get(t) == i:
                        emit_final_add(t)
            for t in range(NOT):
                if FIN_AT.get(t, 99) > NSL - 1:
                    emit_final_add(t)

        @block.scalar
        def _(sc):
            ss = [0]
            for t in range(NT):
                sc.wait_ge(SM["s_px0" if t % 2 == 0 else "s_px1"], (t // 2 + 1) * 16)
                if t >= 1:
                    sc.wait_ge(SM["s_sq"], t)
                sc.activation(sqdump[:], xbuf[:, (t % 2) * D:(t % 2 + 1) * D],
                              AF.Square, accum_out=sumsq[:, t:t + 1]
                              ).then_inc(SM["s_sq"], 1)
            sc.wait_ge(SM["s_vv"], 5)
            sc.activation(lnv[:, 0:NT], vareps[:, 0:NT], AF.Ln).then_inc(SM["s_ss"], 1)
            sc.wait_ge(SM["s_ss"], 1)
            sc.activation(rstd[:, 0:NT], lnv[:, 0:NT], AF.Exp, scale=-0.5
                          ).then_inc(SM["s_ss"], 1)
            ss[0] = 2
            for t in range(NT):
                sc.wait_ge(SM["s_tpe"], t + 1)
                sc.dma_start(out=xlnT_dram.ap()[t].transpose([1, 0, 2]),
                             in_=xlnT_stage[:, (t % 2) * D:(t % 2 + 1) * D]
                             ).then_inc(SM["s_xtd0" if t % 2 == 0 else "s_xtd1"], 16)
                if t >= 2 and t % 2 == 1:
                    n = (t - 2) // 2
                    sc.wait_ge(SM["s_tp"], 2 * n + 4)
                    if n >= 1:
                        sc.wait_ge(SM["s_poold"], n * 16)
                    sc.activation(pool_stage[:, 0:512], psD[0:1, 0:512],
                                  AF.Copy, scale=1.0 / 256)
                    sc.activation(pool_stage[:, 512:1024], psD[0:1, 512:1024],
                                  AF.Copy, scale=1.0 / 256).then_inc(SM["s_poolev"], 1)
            for gq in range(NQG):
                gb = gq % 2
                for w in range(8):
                    wp = w % 2
                    for jt in range(8):
                        tau = 64 * gq + w * 8 + jt
                        sc.wait_ge(SM["s_sc"], tau + 1)
                        if tau >= 16:
                            sc.wait_ge(SM["s_pv"], (tau - 16) // 8 + 1)
                        sc.activation(
                            p_wave[:, (wp * 8 + jt) * 256:(wp * 8 + jt + 1) * 256],
                            sc_slot(tau), AF.Exp, scale=0.125,
                            bias=mexp_sb[:, 0:1]).then_inc(SM["s_pexp"], 1)
                    sc.wait_ge(SM["s_pv"], 8 * gq + w + 1)
                    sc.wait_ge(SM["s_pvev"], 8 * gq + w + 1)
                    if w == 0:
                        sc.wait_ge(SM["s_sd0"], gq * 16)
                        sc.wait_ge(SM["s_sd1"], gq * 16)
                    sc.activation(stage_sums[0:1, w * 256:(w + 1) * 256],
                                  psC[0:1, (w % 2) * 512 + 256:(w % 2) * 512 + 512],
                                  AF.Copy).then_inc(SM["s_pvea"], 1)
                    sc.activation(stage_sums[64:65, w * 256:(w + 1) * 256],
                                  psC[64:65, (w % 2) * 512 + 256:(w % 2) * 512 + 512],
                                  AF.Copy).then_inc(SM["s_pvea"], 1)
                sc.wait_ge(SM["s_gt"], gq + 1)
                sc.activation(gates_bf[:], psD[0:16, 0:512], AF.Sigmoid
                              ).then_inc(SM["s_gte"], 1)
                for pt in range(4):
                    sc.wait_ge(SM["s_lsc"], 8 * gq + 2 * (pt + 1))
                    sc.wait_ge(SM["s_lpv"], 8 * gq)
                    ltcol = (pt % 2) * 512 + (pt // 2) * 256
                    sc.activation(p_lat[:, pt * 256:(pt + 1) * 256],
                                  psD[:, ltcol:ltcol + 256],
                                  AF.Exp, scale=0.125, bias=mexp_sb[:, 0:1]
                                  ).then_inc(SM["s_lexp"], 1)
                for lh in range(8):
                    sc.wait_ge(SM["s_lpv"], 8 * gq + 8)
                    if lh == 0:
                        sc.wait_ge(SM["s_sd2"], gq * 16)
                        sc.wait_ge(SM["s_sd3"], gq * 16)
                    row = (lh % 2) * 64
                    sc.activation(
                        stage_lat[row:row + 1, (lh // 2) * 256:(lh // 2 + 1) * 256],
                        psB[row:row + 1, (lh // 2) * 256:(lh // 2 + 1) * 256],
                        AF.Copy).then_inc(SM["s_lpve"], 1)
                for jx in range(20):
                    sc.wait_ge(SM["s_ex"], 20 * gq + jx + 1)
                    dst = (g_loc[:, (jx % 8) * 256:(jx % 8 + 1) * 256] if jx < 8 else
                           g_gr[:, ((jx - 8) % 8) * 256:((jx - 8) % 8 + 1) * 256]
                           if jx < 16 else
                           g_rn[:, (jx - 16) * 256:(jx - 15) * 256])
                    sc.activation(dst, ex_slot(jx), AF.Copy).then_inc(SM["s_exe"], 1)
                for sub in range(2):
                    t2 = 2 * gq + sub
                    sc.wait_ge(SM["s_rese"], t2 + 1)
                    sc.dma_start(out=xres_dram.ap()[t2 * 128:(t2 + 1) * 128, :],
                                 in_=xbuf[:, sub * D:(sub + 1) * D]
                                 ).then_inc(SM["s_out" if sub == 0 else "s_out1"], 16)
                    sc.wait_ge(SM["s_sq"], NT + t2)
                    sc.activation(sqdump[:], xbuf[:, sub * D:(sub + 1) * D],
                                  AF.Square, accum_out=sumsq[:, t2:t2 + 1]
                                  ).then_inc(SM["s_sq"], 1)
                sc.wait_ge(SM["s_ln2"], 7 * gq + 1)
                sl2 = slice(2 * gq, 2 * gq + 2)
                sc.activation(lnv[:, sl2], vareps[:, sl2], AF.Ln).then_inc(SM["s_ss"], 1)
                ss[0] += 1
                sc.wait_ge(SM["s_ss"], ss[0])
                sc.activation(rstd[:, sl2], lnv[:, sl2], AF.Exp, scale=-0.5
                              ).then_inc(SM["s_ss"], 1)
                ss[0] += 1
                for sub in range(2):
                    t2 = 2 * gq + sub
                    sc.wait_ge(SM["s_tpe2"], t2 + 1)
                    h_ = t2 // 8
                    tt2 = t2 % 8
                    sc.dma_start(
                        out=cc_x_inh[h_].ap()[:, tt2 * 128:(tt2 + 1) * 128]
                        .rearrange("(c p) u -> p c u", p=128),
                        in_=xlnT_stage[:, sub * D:(sub + 1) * D]
                        ).then_inc(SM["s_x2d0" if t2 % 2 == 0 else "s_x2d1"], 16)
                    sc.wait_ge(SM["s_ln2"], 7 * gq + 4 + 2 * sub)
                    sc.activation(e8buf[:, sub * 8:(sub + 1) * 8],
                                  psD[:, 512 + sub * 8:520 + sub * 8],
                                  AF.Exp, bias=negmax8[:, sub:sub + 1]
                                  ).then_inc(SM["s_lge"], 1)
                    sc.wait_ge(SM["s_ln2"], 7 * gq + 5 + 2 * sub)
                    sc.dma_start(out=cc_m_in.ap()[t2 * 128:(t2 + 1) * 128, :],
                                 in_=m_sb[:, sub * 8:(sub + 1) * 8]
                                 ).then_inc(SM["s_md0" if t2 % 2 == 0 else "s_md1"], 16)
            sc.wait_ge(SM["s_out"], NQG * 16)
            sc.wait_ge(SM["s_out1"], NQG * 16)
            sc.wait_ge(SM["s_x2d0"], 8 * 16)
            sc.wait_ge(SM["s_x2d1"], 8 * 16)
            for sdn in ("s_sd0", "s_sd1", "s_sd2", "s_sd3"):
                sc.wait_ge(SM[sdn], NQG * 16)
            def emit_final_store(t):
                sc.wait_ge(SM["s_fadd"], t + 1)
                sc.dma_start(out=ydr[t * 128:(t + 1) * 128, :],
                             in_=fxr_sb[:, 0:D]
                             ).then_inc(SM["s_y0" if t % 2 == 0 else "s_y1"], 16)

            for i in range(NSL):
                k, r = divmod(i, NCORES)
                for hc in range(0, NHC, 2):
                    sc.wait_ge(SM["s_hmm"], i * NHC + hc + 1)
                    sc.activation(
                        hid[:, hc * KSLICE:(hc + 1) * KSLICE],
                        psA[:, 0:512],
                        AF.Relu, bias=b1_sb[:, hc:hc + 1]).then_inc(SM["s_hev_e"], 1)
                for g_ in range(4):
                    idx = i * 4 + g_
                    sc.wait_ge(SM["s_oev"], idx * 2 + 2)
                    ob = idx % 2
                    sc.dma_start(
                        out=rs_in[2 * k + g_ // 2].ap()[
                            r, (g_ % 2) * 128:(g_ % 2 + 1) * 128, :],
                        in_=outsb[:, ob * D:(ob + 1) * D]
                        ).then_inc(SM["s_od0" if g_ < 2 else "s_od1"], 16)
                for t in range(NOT):
                    if FIN_AT.get(t) == i:
                        emit_final_store(t)
            for t in range(NOT):
                if FIN_AT.get(t, 99) > NSL - 1:
                    emit_final_store(t)
            sc.wait_ge(SM["s_y0"], NOT // 2 * 16)
            sc.wait_ge(SM["s_y1"], NOT // 2 * 16)

        @block.tensor
        def _(te):
            te.wait_ge(SM["s_wc"], 5 * 16)
            te.wait_ge(SM["s_vv"], 1)
            for t in range(NT):
                te.wait_ge(SM["s_ln"], t + 1)
                if t >= 2:
                    te.wait_ge(SM["s_tpe"], t - 1)
                dstp = psA_bf if t % 2 == 0 else psB_bf
                tp_last = None
                for c in range(NDC):
                    tp_last = te.transpose(
                        dstp[:, c * 128:(c + 1) * 128],
                        xlnbuf[:, (t % 2) * D + c * 128:(t % 2) * D + (c + 1) * 128],
                        id_sb[:])
                if t >= 2:
                    n = (t - 2) // 2
                    if t % 2 == 0 and n >= 1:
                        te.wait_ge(SM["s_poolev"], n)
                    for half in range(2):
                        mm = te.matmul(psD[0:1, half * 512:(half + 1) * 512],
                                       onescol[:, 0:1],
                                       xlnbuf[:, (t % 2) * D + half * 512:
                                              (t % 2) * D + (half + 1) * 512],
                                       start=(t % 2 == 0), stop=(t % 2 == 1))
                    tp_last = mm
                tp_last.then_inc(SM["s_tp"], 1)
            te.wait_ge(SM["s_w5"], NDC * 16)
            te.wait_ge(SM["s_w6"], NDC * 16)
            te.wait_ge(SM["s_lp"], 1)
            te.wait_ge(SM["s_tpe"], NT)
            tp_last = None
            for c in range(NDC):
                tp_last = te.transpose(psA_bf[:, c * 64:(c + 1) * 64],
                                       pooled_bf[:, c * 128:(c + 1) * 128],
                                       id_sb[0:64, 0:64])
            tp_last.then_inc(SM["s_lp"], 1)
            te.wait_ge(SM["s_lp"], 3)
            for co in range(4):
                for ci in range(NDC):
                    mm = te.matmul(psB[:, co * 64:(co + 1) * 64],
                                   wgk_sb[:, ci * LD + co * 128:ci * LD + (co + 1) * 128],
                                   pooledT[:, ci * 64:(ci + 1) * 64],
                                   start=(ci == 0), stop=(ci == NDC - 1))
            mm.then_inc(SM["s_lp"], 1)
            te.wait_ge(SM["s_lp"], 5)
            te.wait_ge(SM["s_poolev"], 8)
            for ci in range(NDC):
                mm = te.matmul(psC[0:64, 0:512],
                               pooledT[:, ci * 64:(ci + 1) * 64],
                               wgv_sb[:, ci * LD:(ci + 1) * LD],
                               start=(ci == 0), stop=(ci == NDC - 1))
            mm.then_inc(SM["s_lp"], 1)
            def emit_glue(gq2):
                    te.wait_ge(SM["s_prje"], PRJ_TOT[min(gq2 + 2, NQG)])
                    for sub in range(2):
                        t2 = 2 * gq2 + sub
                        te.wait_ge(SM["s_ln2"], 7 * gq2 + 2 + sub)
                        te.wait_ge(SM["s_rese"], t2 + 1)
                        dstb = psA_bf if sub == 0 else psB_bf
                        tp_last = None
                        for c in range(NDC):
                            tp_last = te.transpose(
                                dstb[:, c * 128:(c + 1) * 128],
                                xlnbuf[:, sub * D + c * 128:sub * D + (c + 1) * 128],
                                id_sb[:])
                        tp_last.then_inc(SM["s_tp2"], 1)
                        if gq2 == 0 and sub == 0:
                            te.wait_ge(SM["s_w8"], 14 * 16)
                        te.wait_ge(SM["s_tpe2"], t2 + 1)
                        if gq2 > 0 or sub > 0:
                            te.wait_ge(SM["s_lge"], t2)
                        for ci in range(NDC):
                            te.matmul(psD[:, 512 + sub * 8:520 + sub * 8],
                                          xlnT_stage[:, sub * D + ci * 128:sub * D + (ci + 1) * 128],
                                          gw_sb[:, ci * 8:(ci + 1) * 8],
                                          start=(ci == 0), stop=False)
                        te.matmul(psD[:, 512 + sub * 8:520 + sub * 8],
                                  onesrow[:, 0:128],
                                  gb_sb[:],
                                  start=False, stop=True).then_inc(SM["s_lgm"], 1)

            for gq in range(NQG):
                gb = gq % 2
                if gq == 0:
                    te.wait_ge(SM["s_w1"], NDC * 16)
                    te.wait_ge(SM["s_w2"], NDC * 16)
                te.wait_ge(SM["s_xlnt"], (4 + 2 * gq) * 16)
                te.wait_ge(SM["s_rese"], 2 * gq)
                if gq >= 1:
                    te.wait_ge(SM["s_tpe2"], 2 * (gq - 1))
                j = PRJ_TOT[gq]
                for p in pairs_of(gq):
                    pb = p % 2
                    for co in range(NDC):
                        if j >= 4:
                            te.wait_ge(SM["s_prje"], j - 3)
                        for ci in range(NDC):
                            mm = te.matmul(
                                prj_slot(j)[:, 0:256],
                                wk_sb[:, ci * D + co * 128:ci * D + (co + 1) * 128],
                                xlnT_buf[:, pb * 2048 + ci * 256:pb * 2048 + (ci + 1) * 256],
                                start=(ci == 0), stop=(ci == NDC - 1))
                        mm.then_inc(SM["s_prj"], 1)
                        j += 1
                    for sub in range(2):
                        for dhf in range(2):
                            if j >= 4:
                                te.wait_ge(SM["s_prje"], j - 3)
                            for ci in range(NDC):
                                mm = te.matmul(
                                    prj_slot(j),
                                    xlnT_buf[:, pb * 2048 + ci * 256 + sub * 128:
                                             pb * 2048 + ci * 256 + (sub + 1) * 128],
                                    wv_sb[:, ci * D + dhf * 512:ci * D + (dhf + 1) * 512],
                                    start=(ci == 0), stop=(ci == NDC - 1))
                            mm.then_inc(SM["s_prj"], 1)
                            j += 1
                qpb = (gq + 1) % 2
                if gq == 0:
                    te.wait_ge(SM["s_w0"], NDC * 16)
                for co in range(NDC):
                    if j >= 4:
                        te.wait_ge(SM["s_prje"], j - 3)
                    for ci in range(NDC):
                        mm = te.matmul(
                            prj_slot(j)[:, 0:256],
                            wq_sb[:, ci * D + co * 128:ci * D + (co + 1) * 128],
                            xlnT_buf[:, qpb * 2048 + ci * 256:qpb * 2048 + (ci + 1) * 256],
                            start=(ci == 0), stop=(ci == NDC - 1))
                    mm.then_inc(SM["s_prj"], 1)
                    j += 1
                if gq == 0:
                    te.wait_ge(SM["s_w4"], NDC * 16)
                for co in range(4):
                    if j >= 4:
                        te.wait_ge(SM["s_prje"], j - 3)
                    for ci in range(NDC):
                        mm = te.matmul(
                            prj_slot(j)[:, 0:256],
                            wgq_sb[:, ci * LD + co * 128:ci * LD + (co + 1) * 128],
                            xlnT_buf[:, qpb * 2048 + ci * 256:qpb * 2048 + (ci + 1) * 256],
                            start=(ci == 0), stop=(ci == NDC - 1))
                    mm.then_inc(SM["s_prj"], 1)
                    j += 1
                if gq >= 1:
                    emit_glue(gq - 1)
                if gq == 0:
                    te.wait_ge(SM["s_w7"], NDC * 16)
                te.wait_ge(SM["s_prje"], PRJ_TOT[gq + 1])
                te.wait_ge(SM["s_mx"], 8 * gq)
                te.wait_ge(SM["s_lge"], 2 * gq)
                te.wait_ge(SM["s_tpe2"], 2 * gq)
                for half in range(2):
                    for ci in range(NDC):
                        mm = te.matmul(psD[0:16, half * 256:(half + 1) * 256],
                                       gbd_sb[:, ci * 32 + half * 16:ci * 32 + (half + 1) * 16],
                                       qt_qg[:, ci * 256:(ci + 1) * 256],
                                       start=(ci == 0), stop=(ci == NDC - 1))
                mm.then_inc(SM["s_gt"], 1)
                for w in range(8):
                    for h2 in range(2):
                        h = 2 * w + h2
                        po = (h % 2) * 64
                        sl = h // 2
                        for kc in range(4):
                            tau = 64 * gq + w * 8 + h2 * 4 + kc
                            if tau >= 4:
                                te.wait_ge(SM["s_pexp"], tau - 3)
                            ktile = 2 * gq + kc
                            sp = (ktile // 2) % 2
                            te.matmul(
                                sc_slot(tau),
                                kt_roll[po:po + 64,
                                        sp * 2048 + sl * 256 + (ktile % 2) * 128:
                                        sp * 2048 + sl * 256 + (ktile % 2) * 128 + 128],
                                qt_qg[po:po + 64, sl * 256:(sl + 1) * 256],
                                start=True, stop=True,
                                tile_position=(po, 0),
                            ).then_inc(SM["s_sc"], 1)
                    te.wait_ge(SM["s_p"], 64 * gq + 8 * (w + 1))
                    if 8 * gq + w >= 2:
                        te.wait_ge(SM["s_pvea"], 2 * (8 * gq + w - 1))
                    pvb = (w % 2) * 512
                    for h2 in range(2):
                        h = 2 * w + h2
                        po = h2 * 64
                        for kc in range(4):
                            ktile = 2 * gq + kc
                            sp = (ktile // 2) % 2
                            jt = h2 * 4 + kc
                            wp = w % 2
                            te.matmul(
                                psC[po:po + 64, pvb:pvb + 256],
                                v_roll[:, sp * 2048 + (ktile % 2) * 1024 + h * 64:
                                       sp * 2048 + (ktile % 2) * 1024 + (h + 1) * 64],
                                p_wave[:, (wp * 8 + jt) * 256:(wp * 8 + jt + 1) * 256],
                                start=(kc == 0), stop=(kc == 3),
                                tile_position=(0, po),
                            )
                        for kc in range(4):
                            jt = h2 * 4 + kc
                            wp = w % 2
                            mm = te.matmul(
                                psC[po:po + 1, pvb + 256:pvb + 512],
                                onescol[:, 0:1],
                                p_wave[:, (wp * 8 + jt) * 256:(wp * 8 + jt + 1) * 256],
                                start=(kc == 0), stop=(kc == 3),
                                tile_position=(0, po),
                            )
                    mm.then_inc(SM["s_pv"], 1)
                te.wait_ge(SM["s_gte"], gq + 1)
                for lh in range(8):
                    po = (lh % 2) * 64
                    pt = lh // 2
                    ltcol = (pt % 2) * 512 + (pt // 2) * 256
                    if pt >= 2 and lh % 2 == 0:
                        te.wait_ge(SM["s_lexp"], 4 * gq + pt - 1)
                    te.matmul(psD[po:po + 64, ltcol:ltcol + 256],
                              gkt_sb[po:po + 64, (lh // 2) * 64:(lh // 2 + 1) * 64],
                              gqt_qg[po:po + 64, (lh // 2) * 256:(lh // 2 + 1) * 256],
                              start=True, stop=True,
                              tile_position=(po, po)).then_inc(SM["s_lsc"], 1)
                if gq == 0:
                    te.wait_ge(SM["s_gv2d"], 16)
                te.wait_ge(SM["s_pexp"], 64 * (gq + 1))
                te.wait_ge(SM["s_rn"], 4 * gq)
                te.wait_ge(SM["s_sd0"], gq * 16)
                te.wait_ge(SM["s_sd1"], gq * 16)
                for lh in range(8):
                    po = (lh % 2) * 64
                    te.wait_ge(SM["s_lp2"], 4 * gq + lh // 2 + 1)
                    te.matmul(psA[po:po + 64, (lh // 2) * 256:(lh // 2 + 1) * 256],
                              gv2_sb[po:po + 64, lh * 64:(lh + 1) * 64],
                              p_lat[po:po + 64, (lh // 2) * 256:(lh // 2 + 1) * 256],
                              start=True, stop=True, tile_position=(po, po))
                    te.matmul(psB[po:po + 1, (lh // 2) * 256:(lh // 2 + 1) * 256],
                              onescol[po:po + 64, 0:1],
                              p_lat[po:po + 64, (lh // 2) * 256:(lh // 2 + 1) * 256],
                              start=True, stop=True,
                              tile_position=(po, po)).then_inc(SM["s_lpv"], 1)
                te.wait_ge(SM["s_rcp"], 2 * gq + 1)
                te.wait_ge(SM["s_pvev"], 8 * (gq + 1))
                te.wait_ge(SM["s_pvea"], 16 * (gq + 1))
                te.wait_ge(SM["s_lpve"], 8 * (gq + 1))
                for jx in range(20):
                    if jx == 16:
                        te.wait_ge(SM["s_rcp"], 2 * gq + 2)
                        te.wait_ge(SM["s_sd2"], (gq + 1) * 16)
                        te.wait_ge(SM["s_sd3"], (gq + 1) * 16)
                    if jx >= 8:
                        te.wait_ge(SM["s_exe"], 20 * gq + jx - 7)
                    if jx < 8:
                        te.matmul(ex_slot(jx), exl_sb[:, jx * 128:(jx + 1) * 128],
                                  glr_bf[:], start=True, stop=True
                                  ).then_inc(SM["s_ex"], 1)
                    elif jx < 16:
                        te.matmul(ex_slot(jx), exl_sb[:, (jx - 8) * 128:(jx - 7) * 128],
                                  gates_bf[:, 256:512], start=True, stop=True
                                  ).then_inc(SM["s_ex"], 1)
                    else:
                        te.matmul(ex_slot(jx), exr_sb[:, (jx - 16) * 128:(jx - 15) * 128],
                                  recipR_bf[:], start=True, stop=True
                                  ).then_inc(SM["s_ex"], 1)
                if gq == 0:
                    te.wait_ge(SM["s_w8"], 4 * 16)
                te.wait_ge(SM["s_rn"], 4 * (gq + 1))
                te.wait_ge(SM["s_exe"], 20 * (gq + 1))
                te.wait_ge(SM["s_lexp"], 4 * (gq + 1))
                for co in range(NDC):
                    dstp = psC if co < 4 else psD
                    for ci in range(4):
                        mm = te.matmul(dstp[:, (co % 4) * 256:(co % 4 + 1) * 256],
                                       rout_sb[:, ci * D + co * 128:ci * D + (co + 1) * 128],
                                       rem_norm[:, ci * 256:(ci + 1) * 256],
                                       start=(ci == 0), stop=(ci == 3))
                    mm.then_inc(SM["s_ro"], 1)
                if gq == 0:
                    te.wait_ge(SM["s_w3"], NDC * 16)
                for sub in range(2):
                    dstp = psA if sub == 0 else psB
                    for dhf in range(2):
                        for c in range(NDC):
                            if sub == 0 and dhf == 0:
                                te.wait_ge(SM["s_mx"], 8 * gq + c + 1)
                            mm = te.matmul(
                                dstp[:, dhf * 512:(dhf + 1) * 512],
                                mix_sb[:, c * 256 + sub * 128:c * 256 + (sub + 1) * 128],
                                wout_sb[:, c * D + dhf * 512:c * D + (dhf + 1) * 512],
                                start=(c == 0), stop=(c == NDC - 1))
                        mm.then_inc(SM["s_wo"], 1)
            emit_glue(NQG - 1)
            for sname in ("s_f1a", "s_f1b"):
                te.wait_ge(SM[sname], 4 * 16)
            te.wait_ge(SM["s_tpe2"], 2 * NQG)
            for i in range(NSL):
                buf = i % 2
                psem = "s_fxs0" if buf == 0 else "s_fxs1"
                te.wait_ge(SM[psem], ((i // 2) * 12 + NDC) * 16)
                xb = buf * NDC
                for hc in range(NHC):
                    gidx = i * NHC + hc
                    if gidx >= 2:
                        sem = SM["s_hev_e"] if hc % 2 == 0 else SM["s_hev_o"]
                        te.wait_ge(sem, i * 16 + hc // 2)
                    for d_ in range(NDC):
                        mm = te.matmul(
                            psA[:, (hc % 2) * 512:(hc % 2) * 512 + 512],
                            w1sb[:, d_ * HID + hc * 128: d_ * HID + (hc + 1) * 128],
                            xsl[:, (xb + d_) * KSLICE:(xb + d_ + 1) * KSLICE],
                            start=(d_ == 0), stop=(d_ == NDC - 1),
                        )
                    mm.then_inc(SM["s_hmm"], 1)
                te.wait_ge(SM["s_hev_e"], (i + 1) * 16)
                te.wait_ge(SM["s_hev_o"], (i + 1) * 16)
                if i == 0:
                    te.wait_ge(SM["s_lge"], 2 * NQG)
                    for sname in ("s_f2a", "s_f2b", "s_f2c", "s_f2d"):
                        te.wait_ge(SM[sname], 8 * 16)
                for g_ in range(4):
                    idx = i * 4 + g_
                    if idx >= 2:
                        te.wait_ge(SM["s_oev"], (idx - 2) * 2 + 2)
                    for dhf in range(2):
                        bank = (g_ % 2) * 2 + dhf
                        dstp = psC if bank < 2 else psD
                        te.matmul(
                            dstp[:, (bank % 2) * 512:(bank % 2 + 1) * 512],
                            onesrow[:1, 0:128],
                            b2_sb[:1, dhf * 512:(dhf + 1) * 512],
                            start=True, stop=False)
                        for hc in range(NHC):
                            mm = te.matmul(
                                dstp[:, (bank % 2) * 512:(bank % 2 + 1) * 512],
                                hid[:, hc * KSLICE + g_ * 128:hc * KSLICE + (g_ + 1) * 128],
                                w2sb[:, hc * D + dhf * 512:hc * D + (dhf + 1) * 512],
                                start=False, stop=(hc == NHC - 1),
                            )
                        mm.then_inc(SM["s_omm"], 1)

    return nc


def _host_inputs(inp):
    xg = np.asarray(inp["L0_attn_x"]).reshape(S, D).astype(np.float32)
    W = {
        "wq": np.asarray(inp["L0_attn_wq"], np.float32),
        "wk": np.asarray(inp["L0_attn_wk"], np.float32),
        "wv": np.asarray(inp["L0_attn_wv"], np.float32),
        "wgq": np.asarray(inp["L0_attn_wgq"], np.float32),
        "wgk": np.asarray(inp["L0_attn_wgk"], np.float32),
        "wgv": np.asarray(inp["L0_attn_wgv"], np.float32),
        "rout": np.asarray(inp["L0_attn_r_out_w"], np.float32),
        "wout": np.asarray(inp["L0_attn_w_out"], np.float32),
        "glw": np.asarray(inp["L0_attn_gate_local_w"], np.float32),
        "grw": np.asarray(inp["L0_attn_gate_remote_w"], np.float32),
        "gw": np.asarray(inp["L0_ffn_gate_w"], np.float32),
        "gb": np.asarray(inp["L0_ffn_gate_b"], np.float32),
        "w1": np.asarray(inp["L0_ffn_w1"], np.float32),
        "b1": np.asarray(inp["L0_ffn_b1"], np.float32),
        "w2": np.asarray(inp["L0_ffn_w2"], np.float32),
        "b2": np.asarray(inp["L0_ffn_b2"], np.float32),
    }
    gbd = np.zeros((D, 32), np.float32)
    for h in range(H):
        rho = (h % 2) * 8 + h // 2
        gbd[h * DH:(h + 1) * DH, rho] = W["glw"][h]
        gbd[h * DH:(h + 1) * DH, 16 + rho] = W["grw"][h]
    ident = np.eye(128, dtype=ml_dtypes.bfloat16)
    exl = np.zeros((16, NDC * 128), ml_dtypes.bfloat16)
    for c in range(8):
        for dd in range(128):
            h = 2 * c + dd // 64
            exl[(h % 2) * 8 + h // 2, c * 128 + dd] = 1
    exr = np.zeros((8, 4 * 128), ml_dtypes.bfloat16)
    for c in range(4):
        for dd in range(128):
            lh = 2 * c + dd // 64
            exr[(lh % 2) * 4 + lh // 2, c * 128 + dd] = 1
    in_maps = []
    for ccc in range(NCORES):
        t0 = ccc * T
        xh = np.zeros((TH, D), np.float32)
        lo = max(0, t0 - HALO)
        xh[HALO - (t0 - lo):] = xg[lo:t0 + T]
        masks = np.zeros((NQG, 4, 128, 256), ml_dtypes.bfloat16)
        latmask = np.zeros((NQG, 128, 256), ml_dtypes.bfloat16)
        for g in range(NQG):
            qglob = t0 + g * 256 + np.arange(256)[None, :]
            for kc in range(4):
                kglob = t0 - 256 + g * 256 + kc * 128 + np.arange(128)[:, None]
                ok = (kglob >= 0) & (kglob > qglob - 256) & (kglob <= qglob)
                masks[g, kc] = ok.astype(np.float32)
            lat_ok = (np.arange(64)[:, None] <= (qglob[0][None, :] // 256))
            latmask[g, 0:64] = lat_ok.astype(np.float32)
            latmask[g, 64:128] = lat_ok.astype(np.float32)
        oh = np.zeros((128, 8), np.float32)
        oh[:, ccc] = 1.0
        in_maps.append({
            "x": xh, "wq": W["wq"], "wk": W["wk"], "wv": W["wv"],
            "wgq": W["wgq"], "wgk": W["wgk"], "wgv": W["wgv"],
            "rout": W["rout"], "wout": W["wout"], "gbd": gbd,
            "gw": W["gw"], "gbias": W["gb"].reshape(1, 8),
            "w1": W["w1"][ccc], "b1col": np.ascontiguousarray(
                W["b1"][ccc].reshape(NHC, 128).T),
            "w2": W["w2"][ccc], "b2row": W["b2"][ccc].reshape(1, D),
            "onehot": oh, "ident": ident, "exl": exl, "exr": exr,
            "masks": masks, "latmask": latmask,
        })
    return in_maps


def kernel(**inputs):
    in_maps = _host_inputs(inputs)
    nc = build()
    res = run_bass_kernel_spmd(nc, in_maps, core_ids=list(range(NCORES)))
    out = np.concatenate([np.asarray(res.results[c]["y"], dtype=np.float32)
                          for c in range(NCORES)], axis=0)
    return out.reshape(1, S, D)


# version 12
# speedup vs baseline: 1.0106x; 1.0106x over previous
import numpy as np
import ml_dtypes
import concourse.bass as bass
import concourse.mybir as mybir
from concourse.bass_utils import run_bass_kernel_spmd

F32 = mybir.dt.float32
BF16 = mybir.dt.bfloat16
AF = mybir.ActivationFunctionType
ALU = mybir.AluOpType
AXX = mybir.AxisListType.X

NCORES = 8
S = 16384
T = 2048
HALO = 256
TH = T + HALO
NT = TH // 128
NOT = T // 128
D = 1024
NDC = 8
H = 16
DH = 64
LH = 8
LD = 512
NL = 64
NQG = 8
M_EXP = 8.0
EPS = 1e-5
HID = 4096
NHC = HID // 128
KSLICE = 512
NK = T // KSLICE
NSL = NK * NCORES

PRJ_N = [36] + [24] * 7
PRJ_TOT = [0]
for _n in PRJ_N:
    PRJ_TOT.append(PRJ_TOT[-1] + _n)


def pairs_of(g):
    return [0, 1] if g == 0 else [g + 1]


def build():
    nc = bass.Bass("TRN2", target_bir_lowering=False, debug=False)
    xdr = nc.dram_tensor("x", [TH, D], F32, kind="ExternalInput").ap()
    wq = nc.dram_tensor("wq", [D, D], F32, kind="ExternalInput").ap()
    wk = nc.dram_tensor("wk", [D, D], F32, kind="ExternalInput").ap()
    wv = nc.dram_tensor("wv", [D, D], F32, kind="ExternalInput").ap()
    wgq = nc.dram_tensor("wgq", [D, LD], F32, kind="ExternalInput").ap()
    wgk = nc.dram_tensor("wgk", [D, LD], F32, kind="ExternalInput").ap()
    wgv = nc.dram_tensor("wgv", [D, LD], F32, kind="ExternalInput").ap()
    routd = nc.dram_tensor("rout", [LD, D], F32, kind="ExternalInput").ap()
    woutd = nc.dram_tensor("wout", [D, D], F32, kind="ExternalInput").ap()
    gbdd = nc.dram_tensor("gbd", [D, 32], F32, kind="ExternalInput").ap()
    gwd = nc.dram_tensor("gw", [D, 8], F32, kind="ExternalInput").ap()
    gbias = nc.dram_tensor("gbias", [1, 8], F32, kind="ExternalInput").ap()
    w1d = nc.dram_tensor("w1", [D, HID], F32, kind="ExternalInput").ap()
    b1d = nc.dram_tensor("b1col", [128, NHC], F32, kind="ExternalInput").ap()
    w2d = nc.dram_tensor("w2", [HID, D], F32, kind="ExternalInput").ap()
    b2d = nc.dram_tensor("b2row", [1, D], F32, kind="ExternalInput").ap()
    onehotd = nc.dram_tensor("onehot", [128, 8], F32, kind="ExternalInput").ap()
    identd = nc.dram_tensor("ident", [128, 128], BF16, kind="ExternalInput").ap()
    exld = nc.dram_tensor("exl", [16, NDC * 128], BF16, kind="ExternalInput").ap()
    exrd = nc.dram_tensor("exr", [8, 4 * 128], BF16, kind="ExternalInput").ap()
    masksd = nc.dram_tensor("masks", [NQG, 4, 128, 256], BF16,
                            kind="ExternalInput").ap()
    latmaskd = nc.dram_tensor("latmask", [NQG, 128, 256], BF16,
                              kind="ExternalInput").ap()
    ydr = nc.dram_tensor("y", [T, D], F32, kind="ExternalOutput").ap()

    xlnT_dram = nc.dram_tensor("xlnT_dram", [NT, NDC, 128, 128], BF16)
    xres_dram = nc.dram_tensor("xres_dram", [T, D], F32)
    cc_pool_in = nc.dram_tensor("cc_pool_in", [8, D], F32)
    cc_pool_out = nc.dram_tensor("cc_pool_out", [NL, D], F32)
    cc_x_inh = [nc.dram_tensor(f"cc_x_in{h_}", [D, T // 2], BF16)
                for h_ in range(2)]
    cc_x_outh = [nc.dram_tensor(f"cc_x_out{h_}", [NCORES, D, T // 2], BF16)
                 for h_ in range(2)]
    cc_m_in = nc.dram_tensor("cc_m_in", [T, 8], F32)
    cc_m_outh = [nc.dram_tensor(f"cc_m_out{h_}", [NCORES, T // 2, 8], F32)
                 for h_ in range(2)]
    rs_in = [nc.dram_tensor(f"rs_in{k}", [NCORES, KSLICE // 2, D], F32)
             for k in range(2 * NK)]
    rs_out = [nc.dram_tensor(f"rs_out{k}", [KSLICE // 2, D], F32)
              for k in range(2 * NK)]

    from contextlib import ExitStack
    es = ExitStack()
    with es:
        MAPBASE = 16512
        nc.alloc_sbuf_tensor("reserved_map", [128, 51760], F32)

        def at(name, shape, dt, off):
            return nc.alloc_sbuf_tensor_at(name, shape, dt, offset=MAPBASE + off)

        id_sb = at("id_sb", [128, 128], BF16, 0)
        onescol = at("onescol", [128, 1], BF16, 256)
        onesrow = at("onesrow", [1, 128], BF16, 288)
        mexp_sb = at("mexp_sb", [128, 1], F32, 544)
        exl_sb = at("exl_sb", [16, NDC * 128], BF16, 576)
        exr_sb = at("exr_sb", [8, 4 * 128], BF16, 2624)
        gbd_sb = at("gbd_sb", [128, NDC * 32], BF16, 3648)
        gw_sb = at("gw_sb", [128, NDC * 8], BF16, 4160)
        b1_sb = at("b1_sb", [128, NHC], F32, 4288)
        b2_sb = at("b2_sb", [1, D], BF16, 4416)
        gb_sb = at("gb_sb", [1, 8], BF16, 6464)
        onehot_sb = at("onehot_sb", [128, 8], F32, 6496)
        gv2_sb = at("gv2_sb", [128, LD], BF16, 6528)
        gkt_sb = at("gkt_sb", [128, 4 * 64], BF16, 7552)
        pooledT = at("pooledT", [128, NDC * 64], BF16, 8064)
        pool_stage = at("pool_stage", [1, D], F32, 202944)
        wq_sb = at("wq_sb", [128, NDC * D], BF16, 10240)
        wk_sb = at("wk_sb", [128, NDC * D], BF16, 26624)
        wv_sb = at("wv_sb", [128, NDC * D], BF16, 43008)
        wout_sb = at("wout_sb", [128, NDC * D], BF16, 59392)
        wgq_sb = at("wgq_sb", [128, NDC * LD], BF16, 75776)
        wgk_sb = at("wgk_sb", [128, NDC * LD], BF16, 83968)
        wgv_sb = at("wgv_sb", [128, NDC * LD], BF16, 92160)
        rout_sb = at("rout_sb", [128, 4 * D], BF16, 100352)
        kt_roll = at("kt_roll", [128, 2 * 2048], BF16, 108544)
        v_roll = at("v_roll", [128, 2 * 2048], BF16, 116736)
        xlnT_buf = at("xlnT_buf", [128, 2 * 2048], BF16, 124928)
        qt_qg = at("qt_qg", [128, NDC * 256], BF16, 133120)
        gqt_qg = at("gqt_qg", [128, 4 * 256], BF16, 137216)
        p_wave = at("p_wave", [128, 2 * 8 * 256], BF16, 139264)
        mask_sb = at("mask_sb", [128, 2 * 4 * 256], BF16, 147456)
        latmask_sb = at("latmask_sb", [128, 2 * 256], BF16, 151552)
        p_lat = at("p_lat", [128, 4 * 256], BF16, 152576)
        local_sb = at("local_sb", [128, 8 * 256], BF16, 154624)
        stage_sums = at("stage_sums", [128, 8 * 256], F32, 158720)
        stage_lat = at("stage_lat", [128, 4 * 256], F32, 166912)
        recipL_raw = at("recipL_raw", [16, 256], F32, 171008)
        recipR_raw = at("recipR_raw", [8, 256], F32, 172032)
        recipR_bf = at("recipR_bf", [8, 256], BF16, 173056)
        gates_bf = at("gates_bf", [16, 512], BF16, 173568)
        glr_bf = at("glr_bf", [16, 256], BF16, 174592)
        xbuf = at("xbuf", [128, 2 * D], F32, 175104)
        xlnbuf = at("xlnbuf", [128, 2 * D], BF16, 183296)
        xlnT_stage = at("xlnT_stage", [128, 2 * D], BF16, 187392)
        sqdump = at("sqdump", [128, D], F32, 191488)
        negsum = at("negsum", [128, NT], F32, 195584)
        sumsq = at("sumsq", [128, NT], F32, 195712)
        negmean = at("negmean", [128, NT], F32, 195840)
        msq = at("msq", [128, NT], F32, 195968)
        vareps = at("vareps", [128, NT], F32, 196096)
        lnv = at("lnv", [128, NT], F32, 196224)
        rstd = at("rstd", [128, NT], F32, 196352)
        pooled_sb = at("pooled_sb", [64, D], F32, 196480)
        pooled_bf = at("pooled_bf", [64, D], BF16, 200576)
        e8buf = at("e8buf", [128, 16], F32, 202624)
        m_sb = at("m_sb", [128, 2 * 8], F32, 202688)
        srt8 = at("srt8", [128, 8], F32, 202752)
        sel8 = at("sel8", [128, 8], F32, 202784)
        negmax8 = at("negmax8", [128, 2], F32, 202816)
        s8c = at("s8c", [128, 2], F32, 202848)
        r8c = at("r8c", [128, 2], F32, 202880)
        g_loc = at("g_loc", [128, 8 * 256], BF16, 83968)
        g_gr = at("g_gr", [128, 8 * 256], BF16, 88064)
        g_rn = at("g_rn", [128, 4 * 256], BF16, 92160)
        rem_norm = at("rem_norm", [128, 4 * 256], BF16, 94208)
        mix_sb = at("mix_sb", [128, 8 * 256], BF16, 96256)
        tmp1 = stage_sums.ap()
        tmp2 = p_wave.ap().bitcast(F32)
        w1sb = at("w1sb", [128, NDC * HID], BF16, 10240)
        w2sb = at("w2sb", [128, NHC * D], BF16, 75776)
        xsl = at("xsl", [128, 2 * NDC * KSLICE], BF16, 141312)
        hid = at("hid", [128, NHC * KSLICE], BF16, 157696)
        outsb = at("outsb", [128, 2 * D], F32, 190464)
        mt_sb = at("mt_sb", [128, 2 * 4 * 8], F32, 198656)
        mcol = at("mcol", [128, 2 * 4], F32, 198912)
        mdump = at("mdump", [128, 64], F32, 202944)
        fxr_sb = at("fxr_sb", [128, D], F32, 203200)
        ffs_sb = at("ffs_sb", [128, D], F32, 207296)

        psA = es.enter_context(nc.psum_tensor("psA", [128, 1024], F32))
        psB = es.enter_context(nc.psum_tensor("psB", [128, 1024], F32))
        psC = es.enter_context(nc.psum_tensor("psC", [128, 1024], F32))
        psD = es.enter_context(nc.psum_tensor("psD", [128, 1024], F32))
        psA_bf = psA.ap().bitcast(BF16)
        psB_bf = psB.ap().bitcast(BF16)

        names = ("s_pvev s_pvea s_lpd s_gv2d s_w0 s_w1 s_w2 s_w3 s_w4 s_w5 "
                 "s_w6 s_w7 s_w8 s_wc s_sd0 s_sd1 s_sd2 s_sd3 s_px0 s_px1 "
                 "s_xlnt s_mask s_latmask s_xb s_xtd0 s_xtd1 s_st1 s_sq s_vv "
                 "s_ss s_ln s_tp s_tpe s_poolev s_poold s_cc s_lp s_prj "
                 "s_prje s_sc s_pexp s_p s_pv s_gt s_gte s_lsc s_lexp s_lp2 "
                 "s_lpv s_lpve s_rcp s_ex s_exe s_rn s_ro s_mx s_wo s_rese "
                 "s_out s_ln2 s_tp2 s_tpe2 s_x2d0 s_x2d1 s_lgm s_lge s_md0 "
                 "s_md1 s_f1a s_f1b s_f2a s_f2b s_f2c s_f2d s_fxs0 s_fxs1 "
                 "s_hmm s_hev_e s_hev_o s_omm s_oev s_od0 s_od1 s_y0 s_y1 s_fl0 "
                 "s_fl1 s_fadd s_out1").split()
        SM = {n: es.enter_context(nc.semaphore(n)) for n in names}

        FIN_AT = {t: 8 * (t // 4) + 12 + (t % 4) for t in range(NOT)}

        block = es.enter_context(nc.Block())

        def prj_slot(j):
            s = j % 4
            t_ = psA if s < 2 else psB
            return t_[:, (s % 2) * 512:(s % 2) * 512 + 512]

        def sc_slot(tau):
            s = tau % 8
            t_ = psA if (s % 4) < 2 else psB
            col = (s % 2) * 512 + (s // 4) * 256
            return t_[:, col:col + 256]

        def ex_slot(j):
            s = j % 8
            t_ = psB if s < 4 else psC
            return t_[:, (s % 4) * 256:(s % 4) * 256 + 256]

        EARLY_STATE = [None]

        @block.gpsimd
        def _(g):
            wspecs = [(wq_sb, wq, D), (wk_sb, wk, D), (wv_sb, wv, D),
                      (wout_sb, woutd, D), (wgq_sb, wgq, LD), (wgk_sb, wgk, LD),
                      (wgv_sb, wgv, LD), (gbd_sb, gbdd, 32)]
            for ci in range(NDC):
                for wi, (w_sb_, wd, ow) in enumerate(wspecs):
                    if ci >= 1:
                        g.wait_ge(SM[f"s_w{wi}"], ci * 16)
                    g.dma_start(out=w_sb_[:, ci * ow:(ci + 1) * ow],
                                in_=wd[ci * 128:(ci + 1) * 128, :]
                                ).then_inc(SM[f"s_w{wi}"], 16)
            cnt8 = 0
            for ci in range(4):
                if cnt8 >= 1:
                    g.wait_ge(SM["s_w8"], cnt8 * 16)
                g.dma_start(out=rout_sb[:, ci * D:(ci + 1) * D],
                            in_=routd[ci * 128:(ci + 1) * 128, :]
                            ).then_inc(SM["s_w8"], 16)
                cnt8 += 1
            for dst_, src_ in ((b2_sb[:], b2d[:]), (gb_sb[:], gbias[:])):
                g.wait_ge(SM["s_w8"], cnt8 * 16)
                g.dma_start(out=dst_, in_=src_).then_inc(SM["s_w8"], 16)
                cnt8 += 1
            for ci in range(NDC):
                g.wait_ge(SM["s_w8"], cnt8 * 16)
                g.dma_start(out=gw_sb[:, ci * 8:(ci + 1) * 8],
                            in_=gwd[ci * 128:(ci + 1) * 128, :]
                            ).then_inc(SM["s_w8"], 16)
                cnt8 += 1
            for n in range(8):
                g.wait_ge(SM["s_poolev"], n + 1)
                g.dma_start(out=cc_pool_in.ap()[n:n + 1, :],
                            in_=pool_stage[:]).then_inc(SM["s_poold"], 16)
            g.wait_ge(SM["s_poold"], 8 * 16)
            g.collective_compute(
                "AllGather", ALU.bypass, replica_groups=[list(range(NCORES))],
                ins=[cc_pool_in.ap()], outs=[cc_pool_out.ap()]).then_inc(SM["s_cc"], 1)
            g.wait_ge(SM["s_cc"], 1)
            g.dma_start(out=pooled_sb[:], in_=cc_pool_out.ap()[:]
                        ).then_inc(SM["s_lpd"], 16)
            for gq in range(NQG):
                if gq == 4:
                    g.wait_ge(SM["s_x2d0"], 4 * 16)
                    g.wait_ge(SM["s_x2d1"], 4 * 16)
                    g.collective_compute(
                        "AllGather", ALU.bypass,
                        replica_groups=[list(range(NCORES))],
                        ins=[cc_x_inh[0].ap()],
                        outs=[cc_x_outh[0].ap()]).then_inc(SM["s_cc"], 1)
                    g.wait_ge(SM["s_md0"], 4 * 16)
                    g.wait_ge(SM["s_md1"], 4 * 16)
                    g.collective_compute(
                        "AllGather", ALU.bypass,
                        replica_groups=[list(range(NCORES))],
                        ins=[cc_m_in.ap()[0:T // 2, :].opt()],
                        outs=[cc_m_outh[0].ap()]).then_inc(SM["s_cc"], 1)
                g.wait_ge(SM["s_pvev"], (gq + 1) * 8)
                g.wait_ge(SM["s_pvea"], (gq + 1) * 16)
                g.dma_start(out=recipL_raw.ap()[0:8, :],
                            in_=stage_sums[0:1, :]).then_inc(SM["s_sd0"], 16)
                g.dma_start(out=recipL_raw.ap()[8:16, :],
                            in_=stage_sums[64:65, :]).then_inc(SM["s_sd1"], 16)
                g.wait_ge(SM["s_lpve"], (gq + 1) * 8)
                g.dma_start(out=recipR_raw.ap()[0:4, :],
                            in_=stage_lat[0:1, :]).then_inc(SM["s_sd2"], 16)
                g.dma_start(out=recipR_raw.ap()[4:8, :],
                            in_=stage_lat[64:65, :]).then_inc(SM["s_sd3"], 16)
                if gq == 7:
                    g.wait_ge(SM["s_prj"], PRJ_TOT[NQG])
                    f1 = [0, 0]
                    for d_ in range(6):
                        sem = "s_f1a" if d_ % 2 == 0 else "s_f1b"
                        i2 = d_ % 2
                        if f1[i2] >= 1 and f1[i2] % 2 == 0:
                            g.wait_ge(SM[sem], f1[i2] * 16)
                        g.dma_start(out=w1sb[:, d_ * HID:(d_ + 1) * HID],
                                    in_=w1d[d_ * 128:(d_ + 1) * 128, :]
                                    ).then_inc(SM[sem], 16)
                        f1[i2] += 1
                    f2 = [0, 0, 0, 0]
                    for h_ in range(4):
                        i4 = h_ % 4
                        sem = ["s_f2a", "s_f2b", "s_f2c", "s_f2d"][i4]
                        g.dma_start(out=w2sb[:, h_ * D:(h_ + 1) * D],
                                    in_=w2d[h_ * 128:(h_ + 1) * 128, :]
                                    ).then_inc(SM[sem], 16)
                        f2[i4] += 1
                    EARLY_STATE[0] = (f1, f2)
            g.wait_ge(SM["s_out"], NQG * 16)
            g.wait_ge(SM["s_out1"], NQG * 16)
            g.wait_ge(SM["s_wo"], 4 * NQG)
            g.wait_ge(SM["s_mx"], 8 * NQG)
            g.wait_ge(SM["s_ro"], 8 * NQG)
            f1, f2 = EARLY_STATE[0]
            for d_ in range(6, NDC):
                sem = "s_f1a" if d_ % 2 == 0 else "s_f1b"
                i2 = d_ % 2
                if f1[i2] >= 1 and f1[i2] % 2 == 0:
                    g.wait_ge(SM[sem], f1[i2] * 16)
                g.dma_start(out=w1sb[:, d_ * HID:(d_ + 1) * HID],
                            in_=w1d[d_ * 128:(d_ + 1) * 128, :]
                            ).then_inc(SM[sem], 16)
                f1[i2] += 1
            for h_ in range(4, NHC):
                i4 = h_ % 4
                sem = ["s_f2a", "s_f2b", "s_f2c", "s_f2d"][i4]
                if f2[i4] >= 1 and f2[i4] % 2 == 0:
                    g.wait_ge(SM[sem], f2[i4] * 16)
                g.dma_start(out=w2sb[:, h_ * D:(h_ + 1) * D],
                            in_=w2d[h_ * 128:(h_ + 1) * 128, :]
                            ).then_inc(SM[sem], 16)
                f2[i4] += 1
            g.wait_ge(SM["s_x2d0"], 8 * 16)
            g.wait_ge(SM["s_x2d1"], 8 * 16)
            g.collective_compute(
                "AllGather", ALU.bypass,
                replica_groups=[list(range(NCORES))],
                ins=[cc_x_inh[1].ap()],
                outs=[cc_x_outh[1].ap()]).then_inc(SM["s_cc"], 1)
            g.wait_ge(SM["s_md0"], 8 * 16)
            g.wait_ge(SM["s_md1"], 8 * 16)
            g.collective_compute(
                "AllGather", ALU.bypass,
                replica_groups=[list(range(NCORES))],
                ins=[cc_m_in.ap()[T // 2:T, :].opt()],
                outs=[cc_m_outh[1].ap()]).then_inc(SM["s_cc"], 1)
            for k8 in range(2 * NK):
                g.wait_ge(SM["s_od0" if k8 % 2 == 0 else "s_od1"],
                          (8 * (k8 // 2) + 8) * 2 * 16)
                g.collective_compute(
                    "ReduceScatter", ALU.add,
                    replica_groups=[list(range(NCORES))],
                    ins=[rs_in[k8].ap()], outs=[rs_out[k8].ap()]).then_inc(SM["s_cc"], 1)


        @block.sync
        def _(sy):
            sy.dma_start(out=id_sb[:], in_=identd[:]).then_inc(SM["s_wc"], 16)
            sy.dma_start(out=exl_sb[:], in_=exld[:]).then_inc(SM["s_wc"], 16)
            sy.dma_start(out=exr_sb[:], in_=exrd[:]).then_inc(SM["s_wc"], 16)
            sy.dma_start(out=b1_sb[:], in_=b1d[:]).then_inc(SM["s_wc"], 16)
            sy.dma_start(out=onehot_sb[:], in_=onehotd[:]).then_inc(SM["s_wc"], 16)
            for t in range(NT):
                if t >= 2:
                    sy.wait_ge(SM["s_st1"], t - 1)
                    sy.wait_ge(SM["s_sq"], t - 1)
                sy.dma_start(out=xbuf[:, (t % 2) * D:(t % 2 + 1) * D],
                             in_=xdr[t * 128:(t + 1) * 128, :]
                             ).then_inc(SM["s_px0" if t % 2 == 0 else "s_px1"], 16)
            for t in range(NT):
                if t < 2:
                    sy.wait_ge(SM["s_st1"], NT - 1 + t)
                    sy.wait_ge(SM["s_sq"], NT - 1 + t)
                else:
                    sy.wait_ge(SM["s_ln"], t - 1)
                sy.dma_start(out=xbuf[:, (t % 2) * D:(t % 2 + 1) * D],
                             in_=xdr[t * 128:(t + 1) * 128, :]
                             ).then_inc(SM["s_px0" if t % 2 == 0 else "s_px1"], 16)
            xlcnt = [0]
            for gq in range(NQG):
                gb = gq % 2
                if gq >= 1:
                    sy.wait_ge(SM["s_prj"], PRJ_TOT[gq])
                if gq >= 2:
                    sy.wait_ge(SM["s_p"], 64 * (gq - 1))
                    sy.wait_ge(SM["s_lp2"], 4 * (gq - 1))
                for p in pairs_of(gq):
                    pb = p % 2
                    for sub in range(2):
                        t = 2 * p + sub
                        xlcnt[0] += 1
                        if xlcnt[0] > 1:
                            sy.wait_ge(SM["s_xlnt"], (xlcnt[0] - 1) * 16)
                        sy.wait_ge(SM["s_xtd0" if t % 2 == 0 else "s_xtd1"],
                                   (t // 2 + 1) * 16)
                        dst = xlnT_buf.ap()[:, pb * 2048:(pb + 1) * 2048] \
                            .rearrange("a (c u) -> a c u", c=8)[:, :, sub * 128:(sub + 1) * 128]
                        src = xlnT_dram.ap()[t].transpose([1, 0, 2])
                        sy.dma_start(out=dst, in_=src).then_inc(SM["s_xlnt"], 16)
                if gq >= 1:
                    sy.wait_ge(SM["s_mask"], gq * 16)
                sy.dma_start(out=mask_sb[:, gb * 1024:(gb + 1) * 1024],
                             in_=masksd[gq].transpose([1, 0, 2])
                             ).then_inc(SM["s_mask"], 16)
                if gq >= 1:
                    sy.wait_ge(SM["s_latmask"], gq * 16)
                sy.dma_start(out=latmask_sb[:, gb * 256:(gb + 1) * 256],
                             in_=latmaskd[gq]).then_inc(SM["s_latmask"], 16)
                if gq >= 1:
                    sy.wait_ge(SM["s_out"], 16 * gq)
                    sy.wait_ge(SM["s_out1"], 16 * gq)
                    sy.wait_ge(SM["s_ln2"], 7 * gq - 4)
                    sy.wait_ge(SM["s_sq"], NT + 2 * gq)
                for sub in range(2):
                    t = 2 * (gq + 1) + sub
                    if gq == 0 and sub == 0:
                        sy.wait_ge(SM["s_ln"], NT)
                    if gq > 0 or sub > 0:
                        sy.wait_ge(SM["s_xb"], (2 * gq + sub) * 16)
                    sy.dma_start(out=xbuf[:, sub * D:(sub + 1) * D],
                                 in_=xdr[t * 128:(t + 1) * 128, :]
                                 ).then_inc(SM["s_xb"], 16)
                if gq == 0:
                    sy.wait_ge(SM["s_lp"], 7)
                    sy.dma_start(out=gv2_sb[64:128, :], in_=gv2_sb[0:64, :]
                                 ).then_inc(SM["s_gv2d"], 16)
            sy.wait_ge(SM["s_p"], 64 * NQG)
            sy.wait_ge(SM["s_lp2"], 4 * NQG)
            sy.wait_ge(SM["s_lpv"], 8 * NQG)
            sy.wait_ge(SM["s_mx"], 8 * NQG)
            def emit_final_load(t):
                k = t // 4
                sy.wait_ge(SM["s_cc"], 6 + 2 * k + (t % 4) // 2)
                if t >= 1:
                    sy.wait_ge(SM["s_y0" if (t - 1) % 2 == 0 else "s_y1"],
                               ((t - 1) // 2 + 1) * 16)
                    sy.wait_ge(SM["s_fadd"], t)
                sy.wait_ge(SM["s_fl0"], 2 * t * 16)
                sy.dma_start(out=fxr_sb[:, 0:D],
                             in_=xres_dram.ap()[t * 128:(t + 1) * 128, :]
                             ).then_inc(SM["s_fl0"], 16)
                sy.wait_ge(SM["s_fl0"], (2 * t + 1) * 16)
                sy.dma_start(out=ffs_sb[:, 0:D],
                             in_=rs_out[2 * k + (t % 4) // 2].ap()[
                                 ((t % 4) % 2) * 128:((t % 4) % 2 + 1) * 128, :]
                             ).then_inc(SM["s_fl0"], 16)

            for i in range(NSL):
                k, r = divmod(i, NCORES)
                sy.wait_ge(SM["s_cc"], 3 if k < 2 else 5)
                buf = i % 2
                psem = "s_fxs0" if buf == 0 else "s_fxs1"
                if i >= 2:
                    sy.wait_ge(SM["s_hmm"], (i - 1) * NHC)
                fcnt = (i // 2) * 12
                h_ = k // 2
                kk = k % 2
                for d_ in range(NDC):
                    if d_ % 2 == 0 and fcnt > 0:
                        sy.wait_ge(SM[psem], fcnt * 16)
                    sy.dma_start(
                        out=xsl[:, (buf * NDC + d_) * KSLICE:(buf * NDC + d_ + 1) * KSLICE],
                        in_=cc_x_outh[h_].ap()[r, d_ * 128:(d_ + 1) * 128,
                                               kk * KSLICE:(kk + 1) * KSLICE],
                    ).then_inc(SM[psem], 16)
                    fcnt += 1
                for qt in range(4):
                    sy.wait_ge(SM[psem], fcnt * 16)
                    sy.dma_start(
                        out=mt_sb[:, (buf * 4 + qt) * 8:(buf * 4 + qt + 1) * 8],
                        in_=cc_m_outh[h_].ap()[r, kk * KSLICE + qt * 128:
                                               kk * KSLICE + (qt + 1) * 128, :],
                    ).then_inc(SM[psem], 16)
                    fcnt += 1
                for t in range(NOT):
                    if FIN_AT.get(t) == i:
                        emit_final_load(t)
            for t in range(NOT):
                if FIN_AT.get(t, 99) > NSL - 1:
                    emit_final_load(t)

        @block.vector
        def _(v):
            vv = [1]

            def vsync(ins):
                ins.then_inc(SM["s_vv"], 1)
                vv[0] += 1
                v.wait_ge(SM["s_vv"], vv[0])

            v.memset(onescol[:], 1.0)
            v.memset(onesrow[:], 1.0)
            v.memset(mexp_sb[:], -M_EXP).then_inc(SM["s_vv"], 1)
            for t in range(NT):
                v.wait_ge(SM["s_px0" if t % 2 == 0 else "s_px1"], (t // 2 + 1) * 16)
                v.reduce_sum(negsum[:, t:t + 1],
                             xbuf[:, (t % 2) * D:(t % 2 + 1) * D],
                             axis=AXX, negate=True).then_inc(SM["s_st1"], 1)
            v.wait_ge(SM["s_sq"], NT)
            v.wait_ge(SM["s_st1"], NT)
            vsync(v.tensor_scalar(negmean[:, 0:NT], negsum[:, 0:NT], 1.0 / D,
                                  None, op0=ALU.mult))
            vsync(v.tensor_mul(msq[:, 0:NT], negmean[:, 0:NT], negmean[:, 0:NT]))
            vsync(v.scalar_tensor_tensor(vareps[:, 0:NT], sumsq[:, 0:NT], 1.0 / D,
                                         msq[:, 0:NT], op0=ALU.mult,
                                         op1=ALU.subtract))
            v.tensor_scalar(vareps[:, 0:NT], vareps[:, 0:NT], EPS, None,
                            op0=ALU.add).then_inc(SM["s_vv"], 1)
            vv[0] += 1
            for t in range(NT):
                v.wait_ge(SM["s_px0" if t % 2 == 0 else "s_px1"],
                          (NT // 2 + t // 2 + 1) * 16)
                if t == 0:
                    v.wait_ge(SM["s_ss"], 2)
                if t >= 2:
                    v.wait_ge(SM["s_tp"], t - 1)
                v.tensor_scalar(xlnbuf[:, (t % 2) * D:(t % 2 + 1) * D],
                                xbuf[:, (t % 2) * D:(t % 2 + 1) * D],
                                negmean[:, t:t + 1], rstd[:, t:t + 1],
                                op0=ALU.add, op1=ALU.mult).then_inc(SM["s_ln"], 1)
                v.wait_ge(SM["s_tp"], t + 1)
                src = psA_bf if t % 2 == 0 else psB_bf
                if t >= 2:
                    v.wait_ge(SM["s_xtd0" if t % 2 == 0 else "s_xtd1"],
                              (t // 2) * 16)
                v.tensor_copy(xlnT_stage[:, (t % 2) * D:(t % 2 + 1) * D],
                              src[:, 0:1024]).then_inc(SM["s_tpe"], 1)
            v.wait_ge(SM["s_lpd"], 16)
            v.tensor_copy(pooled_bf[:], pooled_sb[:]).then_inc(SM["s_lp"], 1)
            v.wait_ge(SM["s_lp"], 2)
            v.tensor_copy(pooledT[:], psA_bf[:, 0:NDC * 64]).then_inc(SM["s_lp"], 1)
            v.wait_ge(SM["s_lp"], 4)
            v.tensor_copy(gkt_sb[:], psB[:, 0:256]).then_inc(SM["s_lp"], 1)
            v.wait_ge(SM["s_lp"], 6)
            v.tensor_copy(gv2_sb[0:64, :], psC[0:64, 0:512]).then_inc(SM["s_lp"], 1)
            def emit_top2(gq2):
                    for sub in range(2):
                        t2 = 2 * gq2 + sub
                        if gq2 == 0:
                            v.wait_ge(SM["s_xtd0" if sub == 0 else "s_xtd1"],
                                      (NT // 2) * 16)
                        v.wait_ge(SM["s_tp2"], t2 + 1)
                        srcb = psA_bf if sub == 0 else psB_bf
                        if t2 >= 2:
                            v.wait_ge(SM["s_x2d0" if t2 % 2 == 0 else "s_x2d1"],
                                      (t2 // 2) * 16)
                            v.wait_ge(SM["s_lgm"], t2 - 1)
                        v.tensor_copy(xlnT_stage[:, sub * D:(sub + 1) * D],
                                      srcb[:, 0:1024]).then_inc(SM["s_tpe2"], 1)
                    for sub in range(2):
                        t2 = 2 * gq2 + sub
                        v.wait_ge(SM["s_lgm"], t2 + 1)
                        v.reduce_max(negmax8[:, sub:sub + 1],
                                         psD[:, 512 + sub * 8:520 + sub * 8],
                                         axis=AXX, negate=True).then_inc(SM["s_ln2"], 1)
                        v.wait_ge(SM["s_lge"], t2 + 1)
                        vsync(v.reduce_sum(s8c[:, sub:sub + 1],
                                               e8buf[:, sub * 8:(sub + 1) * 8], axis=AXX))
                        vsync(v.reciprocal(r8c[:, sub:sub + 1], s8c[:, sub:sub + 1]))
                        vsync(v.tensor_scalar(e8buf[:, sub * 8:(sub + 1) * 8],
                                                  e8buf[:, sub * 8:(sub + 1) * 8],
                                                  r8c[:, sub:sub + 1], None,
                                                  op0=ALU.mult))
                        vsync(v.max(out=srt8[:], in_=e8buf[:, sub * 8:(sub + 1) * 8]))
                        vsync(v.tensor_scalar(sel8[:], e8buf[:, sub * 8:(sub + 1) * 8],
                                                  srt8[:, 1:2], None, op0=ALU.is_ge))
                        if t2 >= 2:
                            v.wait_ge(SM["s_md0" if t2 % 2 == 0 else "s_md1"],
                                          (t2 // 2) * 16)
                        v.tensor_mul(m_sb[:, sub * 8:(sub + 1) * 8],
                                         e8buf[:, sub * 8:(sub + 1) * 8], sel8[:]
                                         ).then_inc(SM["s_ln2"], 1)
            for gq in range(NQG):
                gb = gq % 2
                j = PRJ_TOT[gq]
                for p in pairs_of(gq):
                    sp = p % 2
                    for co in range(NDC):
                        v.wait_ge(SM["s_prj"], j + 1)
                        v.tensor_copy(
                            kt_roll[:, sp * 2048 + co * 256:sp * 2048 + (co + 1) * 256],
                            prj_slot(j)[:, 0:256]).then_inc(SM["s_prje"], 1)
                        j += 1
                    for sub in range(2):
                        for dhf in range(2):
                            v.wait_ge(SM["s_prj"], j + 1)
                            v.tensor_copy(
                                v_roll[:, sp * 2048 + sub * 1024 + dhf * 512:
                                       sp * 2048 + sub * 1024 + (dhf + 1) * 512],
                                prj_slot(j)).then_inc(SM["s_prje"], 1)
                            j += 1
                for co in range(NDC):
                    v.wait_ge(SM["s_prj"], j + 1)
                    v.tensor_copy(qt_qg[:, co * 256:(co + 1) * 256],
                                  prj_slot(j)[:, 0:256]).then_inc(SM["s_prje"], 1)
                    j += 1
                for co in range(4):
                    v.wait_ge(SM["s_prj"], j + 1)
                    v.tensor_copy(gqt_qg[:, co * 256:(co + 1) * 256],
                                  prj_slot(j)[:, 0:256]).then_inc(SM["s_prje"], 1)
                    j += 1
                if gq >= 1:
                    emit_top2(gq - 1)
                v.wait_ge(SM["s_mask"], (gq + 1) * 16)
                for w in range(8):
                    wp = w % 2
                    for jt in range(8):
                        tau = 64 * gq + w * 8 + jt
                        kc = jt % 4
                        v.wait_ge(SM["s_pexp"], tau + 1)
                        v.tensor_mul(
                            p_wave[:, (wp * 8 + jt) * 256:(wp * 8 + jt + 1) * 256],
                            p_wave[:, (wp * 8 + jt) * 256:(wp * 8 + jt + 1) * 256],
                            mask_sb[:, gb * 1024 + kc * 256:gb * 1024 + (kc + 1) * 256],
                        ).then_inc(SM["s_p"], 1)
                    v.wait_ge(SM["s_pv"], 8 * gq + w + 1)
                    v.tensor_copy(local_sb[:, w * 256:(w + 1) * 256],
                                  psC[:, (w % 2) * 512:(w % 2) * 512 + 256]
                                  ).then_inc(SM["s_pvev"], 1)
                v.wait_ge(SM["s_latmask"], (gq + 1) * 16)
                for pt in range(4):
                    v.wait_ge(SM["s_lexp"], 4 * gq + pt + 1)
                    v.tensor_mul(p_lat[:, pt * 256:(pt + 1) * 256],
                                 p_lat[:, pt * 256:(pt + 1) * 256],
                                 latmask_sb[:, gb * 256:(gb + 1) * 256]
                                 ).then_inc(SM["s_lp2"], 1)
                v.wait_ge(SM["s_sd0"], (gq + 1) * 16)
                v.wait_ge(SM["s_sd1"], (gq + 1) * 16)
                vsync(v.reciprocal(recipL_raw[:], recipL_raw[:]))
                v.wait_ge(SM["s_gte"], gq + 1)
                v.tensor_mul(glr_bf[:], gates_bf[:, 0:256], recipL_raw[:]
                             ).then_inc(SM["s_rcp"], 1)
                v.wait_ge(SM["s_sd2"], (gq + 1) * 16)
                v.wait_ge(SM["s_sd3"], (gq + 1) * 16)
                vsync(v.reciprocal(recipR_raw[:], recipR_raw[:]))
                v.tensor_copy(recipR_bf[:], recipR_raw[:]).then_inc(SM["s_rcp"], 1)
                v.wait_ge(SM["s_exe"], 20 * (gq + 1))
                v.wait_ge(SM["s_lpv"], 8 * (gq + 1))
                for c in range(4):
                    v.tensor_mul(rem_norm[:, c * 256:(c + 1) * 256],
                                 psA[:, c * 256:(c + 1) * 256],
                                 g_rn[:, c * 256:(c + 1) * 256]
                                 ).then_inc(SM["s_rn"], 1)
                last = None
                for c in range(8):
                    last = v.tensor_mul(tmp2[:, c * 256:(c + 1) * 256],
                                        local_sb[:, c * 256:(c + 1) * 256],
                                        g_loc[:, c * 256:(c + 1) * 256])
                vsync(last)
                for half in range(2):
                    last = None
                    for c in range(4 * half, 4 * half + 4):
                        v.wait_ge(SM["s_ro"], 8 * gq + (c // 2) * 2 + 2)
                        srcp = psC if c < 4 else psD
                        last = v.tensor_mul(tmp1[:, c * 256:(c + 1) * 256],
                                            srcp[:, (c % 4) * 256:(c % 4 + 1) * 256],
                                            g_gr[:, c * 256:(c + 1) * 256])
                    vsync(last)
                    for c in range(4 * half, 4 * half + 4):
                        v.tensor_add(mix_sb[:, c * 256:(c + 1) * 256],
                                     tmp1[:, c * 256:(c + 1) * 256],
                                     tmp2[:, c * 256:(c + 1) * 256]
                                     ).then_inc(SM["s_mx"], 1)
                v.wait_ge(SM["s_wo"], 4 * (gq + 1))
                v.wait_ge(SM["s_out"], 16 * gq)
                v.wait_ge(SM["s_out1"], 16 * gq)
                v.wait_ge(SM["s_xb"], (2 * gq + 2) * 16)
                for sub in range(2):
                    srcp = psA if sub == 0 else psB
                    for dhf in range(2):
                        ins_ = v.tensor_add(
                            xbuf[:, sub * D + dhf * 512:sub * D + (dhf + 1) * 512],
                            xbuf[:, sub * D + dhf * 512:sub * D + (dhf + 1) * 512],
                            srcp[:, dhf * 512:(dhf + 1) * 512])
                    ins_.then_inc(SM["s_rese"], 1)
                v.wait_ge(SM["s_rese"], 2 * (gq + 1))
                last = None
                for sub in range(2):
                    t2 = 2 * gq + sub
                    last = v.reduce_sum(negsum[:, t2:t2 + 1],
                                        xbuf[:, sub * D:(sub + 1) * D],
                                        axis=AXX, negate=True)
                vsync(last)
                v.wait_ge(SM["s_sq"], NT + 2 * (gq + 1))
                sl2 = slice(2 * gq, 2 * gq + 2)
                vsync(v.tensor_scalar(negmean[:, sl2], negsum[:, sl2], 1.0 / D,
                                      None, op0=ALU.mult))
                vsync(v.tensor_mul(msq[:, sl2], negmean[:, sl2], negmean[:, sl2]))
                vsync(v.scalar_tensor_tensor(vareps[:, sl2], sumsq[:, sl2],
                                             1.0 / D, msq[:, sl2],
                                             op0=ALU.mult, op1=ALU.subtract))
                v.tensor_scalar(vareps[:, sl2], vareps[:, sl2], EPS, None,
                                op0=ALU.add).then_inc(SM["s_ln2"], 1)
                v.wait_ge(SM["s_ss"], 2 + 2 * (gq + 1))
                for sub in range(2):
                    t2 = 2 * gq + sub
                    v.tensor_scalar(xlnbuf[:, sub * D:(sub + 1) * D],
                                    xbuf[:, sub * D:(sub + 1) * D],
                                    negmean[:, t2:t2 + 1], rstd[:, t2:t2 + 1],
                                    op0=ALU.add, op1=ALU.mult).then_inc(SM["s_ln2"], 1)
            emit_top2(NQG - 1)
            v.wait_ge(SM["s_out"], NQG * 16)
            v.wait_ge(SM["s_out1"], NQG * 16)
            v.wait_ge(SM["s_x2d0"], 8 * 16)
            v.wait_ge(SM["s_x2d1"], 8 * 16)
            for sdn in ("s_sd0", "s_sd1", "s_sd2", "s_sd3"):
                v.wait_ge(SM[sdn], NQG * 16)
            def emit_final_add(t):
                v.wait_ge(SM["s_fl0"], (2 * t + 2) * 16)
                v.tensor_add(fxr_sb[:, 0:D], fxr_sb[:, 0:D],
                             ffs_sb[:, 0:D]).then_inc(SM["s_fadd"], 1)

            for i in range(NSL):
                k, r = divmod(i, NCORES)
                buf = i % 2
                psem = "s_fxs0" if buf == 0 else "s_fxs1"
                v.wait_ge(SM[psem], (i // 2 + 1) * 12 * 16)
                last = None
                for qt in range(4):
                    last = v.scalar_tensor_tensor(
                        mdump[:, (buf * 4 + qt) * 8:(buf * 4 + qt + 1) * 8],
                        mt_sb[:, (buf * 4 + qt) * 8:(buf * 4 + qt + 1) * 8],
                        1.0, onehot_sb[:], op0=ALU.mult, op1=ALU.mult,
                        accum_out=mcol[:, buf * 4 + qt:buf * 4 + qt + 1])
                vsync(last)
                for hc in range(1, NHC, 2):
                    v.wait_ge(SM["s_hmm"], i * NHC + hc + 1)
                    v.tensor_scalar(
                        hid[:, hc * KSLICE:(hc + 1) * KSLICE],
                        psA[:, 512:1024],
                        b1_sb[:, hc:hc + 1], 0.0,
                        op0=ALU.add, op1=ALU.max).then_inc(SM["s_hev_o"], 1)
                for g_ in range(4):
                    idx = i * 4 + g_
                    v.wait_ge(SM["s_omm"], idx * 2 + 2)
                    if idx >= 2:
                        pidx = idx - 2
                        v.wait_ge(SM["s_od0" if (pidx % 4) < 2 else "s_od1"],
                                  ((pidx // 4) * 2 + 2) * 16)
                    ob = idx % 2
                    for dhf in range(2):
                        bank = (g_ % 2) * 2 + dhf
                        srcp = psC if bank < 2 else psD
                        v.tensor_scalar(
                            outsb[:, ob * D + dhf * 512:ob * D + (dhf + 1) * 512],
                            srcp[:, (bank % 2) * 512:(bank % 2 + 1) * 512],
                            mcol[:, buf * 4 + g_:buf * 4 + g_ + 1], None,
                            op0=ALU.mult).then_inc(SM["s_oev"], 1)
                for t in range(NOT):
                    if FIN_AT.get(t) == i:
                        emit_final_add(t)
            for t in range(NOT):
                if FIN_AT.get(t, 99) > NSL - 1:
                    emit_final_add(t)

        @block.scalar
        def _(sc):
            ss = [0]
            for t in range(NT):
                sc.wait_ge(SM["s_px0" if t % 2 == 0 else "s_px1"], (t // 2 + 1) * 16)
                if t >= 1:
                    sc.wait_ge(SM["s_sq"], t)
                sc.activation(sqdump[:], xbuf[:, (t % 2) * D:(t % 2 + 1) * D],
                              AF.Square, accum_out=sumsq[:, t:t + 1]
                              ).then_inc(SM["s_sq"], 1)
            sc.wait_ge(SM["s_vv"], 5)
            sc.activation(lnv[:, 0:NT], vareps[:, 0:NT], AF.Ln).then_inc(SM["s_ss"], 1)
            sc.wait_ge(SM["s_ss"], 1)
            sc.activation(rstd[:, 0:NT], lnv[:, 0:NT], AF.Exp, scale=-0.5
                          ).then_inc(SM["s_ss"], 1)
            ss[0] = 2
            for t in range(NT):
                sc.wait_ge(SM["s_tpe"], t + 1)
                sc.dma_start(out=xlnT_dram.ap()[t].transpose([1, 0, 2]),
                             in_=xlnT_stage[:, (t % 2) * D:(t % 2 + 1) * D]
                             ).then_inc(SM["s_xtd0" if t % 2 == 0 else "s_xtd1"], 16)
                if t >= 2 and t % 2 == 1:
                    n = (t - 2) // 2
                    sc.wait_ge(SM["s_tp"], 2 * n + 4)
                    if n >= 1:
                        sc.wait_ge(SM["s_poold"], n * 16)
                    sc.activation(pool_stage[:, 0:512], psD[0:1, 0:512],
                                  AF.Copy, scale=1.0 / 256)
                    sc.activation(pool_stage[:, 512:1024], psD[0:1, 512:1024],
                                  AF.Copy, scale=1.0 / 256).then_inc(SM["s_poolev"], 1)
            for gq in range(NQG):
                gb = gq % 2
                for w in range(8):
                    wp = w % 2
                    for jt in range(8):
                        tau = 64 * gq + w * 8 + jt
                        sc.wait_ge(SM["s_sc"], tau + 1)
                        if tau >= 16:
                            sc.wait_ge(SM["s_pv"], (tau - 16) // 8 + 1)
                        sc.activation(
                            p_wave[:, (wp * 8 + jt) * 256:(wp * 8 + jt + 1) * 256],
                            sc_slot(tau), AF.Exp, scale=0.125,
                            bias=mexp_sb[:, 0:1]).then_inc(SM["s_pexp"], 1)
                    sc.wait_ge(SM["s_pv"], 8 * gq + w + 1)
                    sc.wait_ge(SM["s_pvev"], 8 * gq + w + 1)
                    if w == 0:
                        sc.wait_ge(SM["s_sd0"], gq * 16)
                        sc.wait_ge(SM["s_sd1"], gq * 16)
                    sc.activation(stage_sums[0:1, w * 256:(w + 1) * 256],
                                  psC[0:1, (w % 2) * 512 + 256:(w % 2) * 512 + 512],
                                  AF.Copy).then_inc(SM["s_pvea"], 1)
                    sc.activation(stage_sums[64:65, w * 256:(w + 1) * 256],
                                  psC[64:65, (w % 2) * 512 + 256:(w % 2) * 512 + 512],
                                  AF.Copy).then_inc(SM["s_pvea"], 1)
                sc.wait_ge(SM["s_gt"], gq + 1)
                sc.activation(gates_bf[:], psD[0:16, 0:512], AF.Sigmoid
                              ).then_inc(SM["s_gte"], 1)
                for pt in range(4):
                    sc.wait_ge(SM["s_lsc"], 8 * gq + 2 * (pt + 1))
                    sc.wait_ge(SM["s_lpv"], 8 * gq)
                    ltcol = (pt % 2) * 512 + (pt // 2) * 256
                    sc.activation(p_lat[:, pt * 256:(pt + 1) * 256],
                                  psD[:, ltcol:ltcol + 256],
                                  AF.Exp, scale=0.125, bias=mexp_sb[:, 0:1]
                                  ).then_inc(SM["s_lexp"], 1)
                for lh in range(8):
                    sc.wait_ge(SM["s_lpv"], 8 * gq + 8)
                    if lh == 0:
                        sc.wait_ge(SM["s_sd2"], gq * 16)
                        sc.wait_ge(SM["s_sd3"], gq * 16)
                    row = (lh % 2) * 64
                    sc.activation(
                        stage_lat[row:row + 1, (lh // 2) * 256:(lh // 2 + 1) * 256],
                        psB[row:row + 1, (lh // 2) * 256:(lh // 2 + 1) * 256],
                        AF.Copy).then_inc(SM["s_lpve"], 1)
                for jx in range(20):
                    sc.wait_ge(SM["s_ex"], 20 * gq + jx + 1)
                    dst = (g_loc[:, (jx % 8) * 256:(jx % 8 + 1) * 256] if jx < 8 else
                           g_gr[:, ((jx - 8) % 8) * 256:((jx - 8) % 8 + 1) * 256]
                           if jx < 16 else
                           g_rn[:, (jx - 16) * 256:(jx - 15) * 256])
                    sc.activation(dst, ex_slot(jx), AF.Copy).then_inc(SM["s_exe"], 1)
                for sub in range(2):
                    t2 = 2 * gq + sub
                    sc.wait_ge(SM["s_rese"], t2 + 1)
                    sc.dma_start(out=xres_dram.ap()[t2 * 128:(t2 + 1) * 128, :],
                                 in_=xbuf[:, sub * D:(sub + 1) * D]
                                 ).then_inc(SM["s_out" if sub == 0 else "s_out1"], 16)
                    sc.wait_ge(SM["s_sq"], NT + t2)
                    sc.activation(sqdump[:], xbuf[:, sub * D:(sub + 1) * D],
                                  AF.Square, accum_out=sumsq[:, t2:t2 + 1]
                                  ).then_inc(SM["s_sq"], 1)
                sc.wait_ge(SM["s_ln2"], 7 * gq + 1)
                sl2 = slice(2 * gq, 2 * gq + 2)
                sc.activation(lnv[:, sl2], vareps[:, sl2], AF.Ln).then_inc(SM["s_ss"], 1)
                ss[0] += 1
                sc.wait_ge(SM["s_ss"], ss[0])
                sc.activation(rstd[:, sl2], lnv[:, sl2], AF.Exp, scale=-0.5
                              ).then_inc(SM["s_ss"], 1)
                ss[0] += 1
                for sub in range(2):
                    t2 = 2 * gq + sub
                    sc.wait_ge(SM["s_tpe2"], t2 + 1)
                    h_ = t2 // 8
                    tt2 = t2 % 8
                    sc.dma_start(
                        out=cc_x_inh[h_].ap()[:, tt2 * 128:(tt2 + 1) * 128]
                        .rearrange("(c p) u -> p c u", p=128),
                        in_=xlnT_stage[:, sub * D:(sub + 1) * D]
                        ).then_inc(SM["s_x2d0" if t2 % 2 == 0 else "s_x2d1"], 16)
                    sc.wait_ge(SM["s_ln2"], 7 * gq + 4 + 2 * sub)
                    sc.activation(e8buf[:, sub * 8:(sub + 1) * 8],
                                  psD[:, 512 + sub * 8:520 + sub * 8],
                                  AF.Exp, bias=negmax8[:, sub:sub + 1]
                                  ).then_inc(SM["s_lge"], 1)
                    sc.wait_ge(SM["s_ln2"], 7 * gq + 5 + 2 * sub)
                    sc.dma_start(out=cc_m_in.ap()[t2 * 128:(t2 + 1) * 128, :],
                                 in_=m_sb[:, sub * 8:(sub + 1) * 8]
                                 ).then_inc(SM["s_md0" if t2 % 2 == 0 else "s_md1"], 16)
            sc.wait_ge(SM["s_out"], NQG * 16)
            sc.wait_ge(SM["s_out1"], NQG * 16)
            sc.wait_ge(SM["s_x2d0"], 8 * 16)
            sc.wait_ge(SM["s_x2d1"], 8 * 16)
            for sdn in ("s_sd0", "s_sd1", "s_sd2", "s_sd3"):
                sc.wait_ge(SM[sdn], NQG * 16)
            def emit_final_store(t):
                sc.wait_ge(SM["s_fadd"], t + 1)
                sc.dma_start(out=ydr[t * 128:(t + 1) * 128, :],
                             in_=fxr_sb[:, 0:D]
                             ).then_inc(SM["s_y0" if t % 2 == 0 else "s_y1"], 16)

            for i in range(NSL):
                k, r = divmod(i, NCORES)
                for hc in range(0, NHC, 2):
                    sc.wait_ge(SM["s_hmm"], i * NHC + hc + 1)
                    sc.activation(
                        hid[:, hc * KSLICE:(hc + 1) * KSLICE],
                        psA[:, 0:512],
                        AF.Relu, bias=b1_sb[:, hc:hc + 1]).then_inc(SM["s_hev_e"], 1)
                for g_ in range(4):
                    idx = i * 4 + g_
                    sc.wait_ge(SM["s_oev"], idx * 2 + 2)
                    ob = idx % 2
                    sc.dma_start(
                        out=rs_in[2 * k + g_ // 2].ap()[
                            r, (g_ % 2) * 128:(g_ % 2 + 1) * 128, :],
                        in_=outsb[:, ob * D:(ob + 1) * D]
                        ).then_inc(SM["s_od0" if g_ < 2 else "s_od1"], 16)
                for t in range(NOT):
                    if FIN_AT.get(t) == i:
                        emit_final_store(t)
            for t in range(NOT):
                if FIN_AT.get(t, 99) > NSL - 1:
                    emit_final_store(t)
            sc.wait_ge(SM["s_y0"], NOT // 2 * 16)
            sc.wait_ge(SM["s_y1"], NOT // 2 * 16)

        @block.tensor
        def _(te):
            te.wait_ge(SM["s_wc"], 5 * 16)
            te.wait_ge(SM["s_vv"], 1)
            for t in range(NT):
                te.wait_ge(SM["s_ln"], t + 1)
                if t >= 2:
                    te.wait_ge(SM["s_tpe"], t - 1)
                dstp = psA_bf if t % 2 == 0 else psB_bf
                tp_last = None
                for c in range(NDC):
                    tp_last = te.transpose(
                        dstp[:, c * 128:(c + 1) * 128],
                        xlnbuf[:, (t % 2) * D + c * 128:(t % 2) * D + (c + 1) * 128],
                        id_sb[:])
                if t >= 2:
                    n = (t - 2) // 2
                    if t % 2 == 0 and n >= 1:
                        te.wait_ge(SM["s_poolev"], n)
                    for half in range(2):
                        mm = te.matmul(psD[0:1, half * 512:(half + 1) * 512],
                                       onescol[:, 0:1],
                                       xlnbuf[:, (t % 2) * D + half * 512:
                                              (t % 2) * D + (half + 1) * 512],
                                       start=(t % 2 == 0), stop=(t % 2 == 1))
                    tp_last = mm
                tp_last.then_inc(SM["s_tp"], 1)
            te.wait_ge(SM["s_w5"], NDC * 16)
            te.wait_ge(SM["s_w6"], NDC * 16)
            te.wait_ge(SM["s_lp"], 1)
            te.wait_ge(SM["s_tpe"], NT)
            tp_last = None
            for c in range(NDC):
                tp_last = te.transpose(psA_bf[:, c * 64:(c + 1) * 64],
                                       pooled_bf[:, c * 128:(c + 1) * 128],
                                       id_sb[0:64, 0:64])
            tp_last.then_inc(SM["s_lp"], 1)
            te.wait_ge(SM["s_lp"], 3)
            for co in range(4):
                for ci in range(NDC):
                    mm = te.matmul(psB[:, co * 64:(co + 1) * 64],
                                   wgk_sb[:, ci * LD + co * 128:ci * LD + (co + 1) * 128],
                                   pooledT[:, ci * 64:(ci + 1) * 64],
                                   start=(ci == 0), stop=(ci == NDC - 1))
            mm.then_inc(SM["s_lp"], 1)
            te.wait_ge(SM["s_lp"], 5)
            te.wait_ge(SM["s_poolev"], 8)
            for ci in range(NDC):
                mm = te.matmul(psC[0:64, 0:512],
                               pooledT[:, ci * 64:(ci + 1) * 64],
                               wgv_sb[:, ci * LD:(ci + 1) * LD],
                               start=(ci == 0), stop=(ci == NDC - 1))
            mm.then_inc(SM["s_lp"], 1)
            def emit_glue(gq2):
                    te.wait_ge(SM["s_prje"], PRJ_TOT[min(gq2 + 2, NQG)])
                    for sub in range(2):
                        t2 = 2 * gq2 + sub
                        te.wait_ge(SM["s_ln2"], 7 * gq2 + 2 + sub)
                        te.wait_ge(SM["s_rese"], t2 + 1)
                        dstb = psA_bf if sub == 0 else psB_bf
                        tp_last = None
                        for c in range(NDC):
                            tp_last = te.transpose(
                                dstb[:, c * 128:(c + 1) * 128],
                                xlnbuf[:, sub * D + c * 128:sub * D + (c + 1) * 128],
                                id_sb[:])
                        tp_last.then_inc(SM["s_tp2"], 1)
                        if gq2 == 0 and sub == 0:
                            te.wait_ge(SM["s_w8"], 14 * 16)
                        te.wait_ge(SM["s_tpe2"], t2 + 1)
                        if gq2 > 0 or sub > 0:
                            te.wait_ge(SM["s_lge"], t2)
                        for ci in range(NDC):
                            te.matmul(psD[:, 512 + sub * 8:520 + sub * 8],
                                          xlnT_stage[:, sub * D + ci * 128:sub * D + (ci + 1) * 128],
                                          gw_sb[:, ci * 8:(ci + 1) * 8],
                                          start=(ci == 0), stop=False)
                        te.matmul(psD[:, 512 + sub * 8:520 + sub * 8],
                                  onesrow[:, 0:128],
                                  gb_sb[:],
                                  start=False, stop=True).then_inc(SM["s_lgm"], 1)

            for gq in range(NQG):
                gb = gq % 2
                if gq == 0:
                    te.wait_ge(SM["s_w1"], NDC * 16)
                    te.wait_ge(SM["s_w2"], NDC * 16)
                te.wait_ge(SM["s_xlnt"], (4 + 2 * gq) * 16)
                te.wait_ge(SM["s_rese"], 2 * gq)
                if gq >= 1:
                    te.wait_ge(SM["s_tpe2"], 2 * (gq - 1))
                j = PRJ_TOT[gq]
                for p in pairs_of(gq):
                    pb = p % 2
                    for co in range(NDC):
                        if j >= 4:
                            te.wait_ge(SM["s_prje"], j - 3)
                        for ci in range(NDC):
                            mm = te.matmul(
                                prj_slot(j)[:, 0:256],
                                wk_sb[:, ci * D + co * 128:ci * D + (co + 1) * 128],
                                xlnT_buf[:, pb * 2048 + ci * 256:pb * 2048 + (ci + 1) * 256],
                                start=(ci == 0), stop=(ci == NDC - 1))
                        mm.then_inc(SM["s_prj"], 1)
                        j += 1
                    for sub in range(2):
                        for dhf in range(2):
                            if j >= 4:
                                te.wait_ge(SM["s_prje"], j - 3)
                            for ci in range(NDC):
                                mm = te.matmul(
                                    prj_slot(j),
                                    xlnT_buf[:, pb * 2048 + ci * 256 + sub * 128:
                                             pb * 2048 + ci * 256 + (sub + 1) * 128],
                                    wv_sb[:, ci * D + dhf * 512:ci * D + (dhf + 1) * 512],
                                    start=(ci == 0), stop=(ci == NDC - 1))
                            mm.then_inc(SM["s_prj"], 1)
                            j += 1
                qpb = (gq + 1) % 2
                if gq == 0:
                    te.wait_ge(SM["s_w0"], NDC * 16)
                for co in range(NDC):
                    if j >= 4:
                        te.wait_ge(SM["s_prje"], j - 3)
                    for ci in range(NDC):
                        mm = te.matmul(
                            prj_slot(j)[:, 0:256],
                            wq_sb[:, ci * D + co * 128:ci * D + (co + 1) * 128],
                            xlnT_buf[:, qpb * 2048 + ci * 256:qpb * 2048 + (ci + 1) * 256],
                            start=(ci == 0), stop=(ci == NDC - 1))
                    mm.then_inc(SM["s_prj"], 1)
                    j += 1
                if gq == 0:
                    te.wait_ge(SM["s_w4"], NDC * 16)
                for co in range(4):
                    if j >= 4:
                        te.wait_ge(SM["s_prje"], j - 3)
                    for ci in range(NDC):
                        mm = te.matmul(
                            prj_slot(j)[:, 0:256],
                            wgq_sb[:, ci * LD + co * 128:ci * LD + (co + 1) * 128],
                            xlnT_buf[:, qpb * 2048 + ci * 256:qpb * 2048 + (ci + 1) * 256],
                            start=(ci == 0), stop=(ci == NDC - 1))
                    mm.then_inc(SM["s_prj"], 1)
                    j += 1
                if gq >= 1:
                    emit_glue(gq - 1)
                if gq == 0:
                    te.wait_ge(SM["s_w7"], NDC * 16)
                te.wait_ge(SM["s_prje"], PRJ_TOT[gq + 1])
                te.wait_ge(SM["s_mx"], 8 * gq)
                te.wait_ge(SM["s_lge"], 2 * gq)
                te.wait_ge(SM["s_tpe2"], 2 * gq)
                for half in range(2):
                    for ci in range(NDC):
                        mm = te.matmul(psD[0:16, half * 256:(half + 1) * 256],
                                       gbd_sb[:, ci * 32 + half * 16:ci * 32 + (half + 1) * 16],
                                       qt_qg[:, ci * 256:(ci + 1) * 256],
                                       start=(ci == 0), stop=(ci == NDC - 1))
                mm.then_inc(SM["s_gt"], 1)
                for w in range(8):
                    for h2 in range(2):
                        h = 2 * w + h2
                        po = (h % 2) * 64
                        sl = h // 2
                        for kc in range(4):
                            tau = 64 * gq + w * 8 + h2 * 4 + kc
                            if tau >= 4:
                                te.wait_ge(SM["s_pexp"], tau - 3)
                            ktile = 2 * gq + kc
                            sp = (ktile // 2) % 2
                            te.matmul(
                                sc_slot(tau),
                                kt_roll[po:po + 64,
                                        sp * 2048 + sl * 256 + (ktile % 2) * 128:
                                        sp * 2048 + sl * 256 + (ktile % 2) * 128 + 128],
                                qt_qg[po:po + 64, sl * 256:(sl + 1) * 256],
                                start=True, stop=True,
                                tile_position=(po, 0),
                            ).then_inc(SM["s_sc"], 1)
                    te.wait_ge(SM["s_p"], 64 * gq + 8 * (w + 1))
                    if 8 * gq + w >= 2:
                        te.wait_ge(SM["s_pvea"], 2 * (8 * gq + w - 1))
                    pvb = (w % 2) * 512
                    for h2 in range(2):
                        h = 2 * w + h2
                        po = h2 * 64
                        for kc in range(4):
                            ktile = 2 * gq + kc
                            sp = (ktile // 2) % 2
                            jt = h2 * 4 + kc
                            wp = w % 2
                            te.matmul(
                                psC[po:po + 64, pvb:pvb + 256],
                                v_roll[:, sp * 2048 + (ktile % 2) * 1024 + h * 64:
                                       sp * 2048 + (ktile % 2) * 1024 + (h + 1) * 64],
                                p_wave[:, (wp * 8 + jt) * 256:(wp * 8 + jt + 1) * 256],
                                start=(kc == 0), stop=(kc == 3),
                                tile_position=(0, po),
                            )
                        for kc in range(4):
                            jt = h2 * 4 + kc
                            wp = w % 2
                            mm = te.matmul(
                                psC[po:po + 1, pvb + 256:pvb + 512],
                                onescol[:, 0:1],
                                p_wave[:, (wp * 8 + jt) * 256:(wp * 8 + jt + 1) * 256],
                                start=(kc == 0), stop=(kc == 3),
                                tile_position=(0, po),
                            )
                    mm.then_inc(SM["s_pv"], 1)
                te.wait_ge(SM["s_gte"], gq + 1)
                for lh in range(8):
                    po = (lh % 2) * 64
                    pt = lh // 2
                    ltcol = (pt % 2) * 512 + (pt // 2) * 256
                    if pt >= 2 and lh % 2 == 0:
                        te.wait_ge(SM["s_lexp"], 4 * gq + pt - 1)
                    te.matmul(psD[po:po + 64, ltcol:ltcol + 256],
                              gkt_sb[po:po + 64, (lh // 2) * 64:(lh // 2 + 1) * 64],
                              gqt_qg[po:po + 64, (lh // 2) * 256:(lh // 2 + 1) * 256],
                              start=True, stop=True,
                              tile_position=(po, po)).then_inc(SM["s_lsc"], 1)
                if gq == 0:
                    te.wait_ge(SM["s_gv2d"], 16)
                te.wait_ge(SM["s_pexp"], 64 * (gq + 1))
                te.wait_ge(SM["s_rn"], 4 * gq)
                te.wait_ge(SM["s_sd0"], gq * 16)
                te.wait_ge(SM["s_sd1"], gq * 16)
                for lh in range(8):
                    po = (lh % 2) * 64
                    te.wait_ge(SM["s_lp2"], 4 * gq + lh // 2 + 1)
                    te.matmul(psA[po:po + 64, (lh // 2) * 256:(lh // 2 + 1) * 256],
                              gv2_sb[po:po + 64, lh * 64:(lh + 1) * 64],
                              p_lat[po:po + 64, (lh // 2) * 256:(lh // 2 + 1) * 256],
                              start=True, stop=True, tile_position=(po, po))
                    te.matmul(psB[po:po + 1, (lh // 2) * 256:(lh // 2 + 1) * 256],
                              onescol[po:po + 64, 0:1],
                              p_lat[po:po + 64, (lh // 2) * 256:(lh // 2 + 1) * 256],
                              start=True, stop=True,
                              tile_position=(po, po)).then_inc(SM["s_lpv"], 1)
                te.wait_ge(SM["s_rcp"], 2 * gq + 1)
                te.wait_ge(SM["s_pvev"], 8 * (gq + 1))
                te.wait_ge(SM["s_pvea"], 16 * (gq + 1))
                te.wait_ge(SM["s_lpve"], 8 * (gq + 1))
                for jx in range(20):
                    if jx == 16:
                        te.wait_ge(SM["s_rcp"], 2 * gq + 2)
                        te.wait_ge(SM["s_sd2"], (gq + 1) * 16)
                        te.wait_ge(SM["s_sd3"], (gq + 1) * 16)
                    if jx >= 8:
                        te.wait_ge(SM["s_exe"], 20 * gq + jx - 7)
                    if jx < 8:
                        te.matmul(ex_slot(jx), exl_sb[:, jx * 128:(jx + 1) * 128],
                                  glr_bf[:], start=True, stop=True
                                  ).then_inc(SM["s_ex"], 1)
                    elif jx < 16:
                        te.matmul(ex_slot(jx), exl_sb[:, (jx - 8) * 128:(jx - 7) * 128],
                                  gates_bf[:, 256:512], start=True, stop=True
                                  ).then_inc(SM["s_ex"], 1)
                    else:
                        te.matmul(ex_slot(jx), exr_sb[:, (jx - 16) * 128:(jx - 15) * 128],
                                  recipR_bf[:], start=True, stop=True
                                  ).then_inc(SM["s_ex"], 1)
                if gq == 0:
                    te.wait_ge(SM["s_w8"], 4 * 16)
                te.wait_ge(SM["s_rn"], 4 * (gq + 1))
                te.wait_ge(SM["s_exe"], 20 * (gq + 1))
                te.wait_ge(SM["s_lexp"], 4 * (gq + 1))
                for co in range(NDC):
                    dstp = psC if co < 4 else psD
                    for ci in range(4):
                        mm = te.matmul(dstp[:, (co % 4) * 256:(co % 4 + 1) * 256],
                                       rout_sb[:, ci * D + co * 128:ci * D + (co + 1) * 128],
                                       rem_norm[:, ci * 256:(ci + 1) * 256],
                                       start=(ci == 0), stop=(ci == 3))
                    mm.then_inc(SM["s_ro"], 1)
                if gq == 0:
                    te.wait_ge(SM["s_w3"], NDC * 16)
                for sub in range(2):
                    dstp = psA if sub == 0 else psB
                    for dhf in range(2):
                        for c in range(NDC):
                            if sub == 0 and dhf == 0:
                                te.wait_ge(SM["s_mx"], 8 * gq + c + 1)
                            mm = te.matmul(
                                dstp[:, dhf * 512:(dhf + 1) * 512],
                                mix_sb[:, c * 256 + sub * 128:c * 256 + (sub + 1) * 128],
                                wout_sb[:, c * D + dhf * 512:c * D + (dhf + 1) * 512],
                                start=(c == 0), stop=(c == NDC - 1))
                        mm.then_inc(SM["s_wo"], 1)
            emit_glue(NQG - 1)
            for sname in ("s_f1a", "s_f1b"):
                te.wait_ge(SM[sname], 4 * 16)
            te.wait_ge(SM["s_tpe2"], 2 * NQG)
            for i in range(NSL):
                buf = i % 2
                psem = "s_fxs0" if buf == 0 else "s_fxs1"
                te.wait_ge(SM[psem], ((i // 2) * 12 + NDC) * 16)
                xb = buf * NDC
                for hc in range(NHC):
                    gidx = i * NHC + hc
                    if gidx >= 2:
                        sem = SM["s_hev_e"] if hc % 2 == 0 else SM["s_hev_o"]
                        te.wait_ge(sem, i * 16 + hc // 2)
                    for d_ in range(NDC):
                        mm = te.matmul(
                            psA[:, (hc % 2) * 512:(hc % 2) * 512 + 512],
                            w1sb[:, d_ * HID + hc * 128: d_ * HID + (hc + 1) * 128],
                            xsl[:, (xb + d_) * KSLICE:(xb + d_ + 1) * KSLICE],
                            start=(d_ == 0), stop=(d_ == NDC - 1),
                        )
                    mm.then_inc(SM["s_hmm"], 1)
                te.wait_ge(SM["s_hev_e"], (i + 1) * 16)
                te.wait_ge(SM["s_hev_o"], (i + 1) * 16)
                if i == 0:
                    te.wait_ge(SM["s_lge"], 2 * NQG)
                    for sname in ("s_f2a", "s_f2b", "s_f2c", "s_f2d"):
                        te.wait_ge(SM[sname], 8 * 16)
                for g_ in range(4):
                    idx = i * 4 + g_
                    if idx >= 2:
                        te.wait_ge(SM["s_oev"], (idx - 2) * 2 + 2)
                    for dhf in range(2):
                        bank = (g_ % 2) * 2 + dhf
                        dstp = psC if bank < 2 else psD
                        te.matmul(
                            dstp[:, (bank % 2) * 512:(bank % 2 + 1) * 512],
                            onesrow[:1, 0:128],
                            b2_sb[:1, dhf * 512:(dhf + 1) * 512],
                            start=True, stop=False)
                        for hc in range(NHC):
                            mm = te.matmul(
                                dstp[:, (bank % 2) * 512:(bank % 2 + 1) * 512],
                                hid[:, hc * KSLICE + g_ * 128:hc * KSLICE + (g_ + 1) * 128],
                                w2sb[:, hc * D + dhf * 512:hc * D + (dhf + 1) * 512],
                                start=False, stop=(hc == NHC - 1),
                            )
                        mm.then_inc(SM["s_omm"], 1)

    return nc


def _host_inputs(inp):
    xg = np.asarray(inp["L0_attn_x"]).reshape(S, D).astype(np.float32)
    W = {
        "wq": np.asarray(inp["L0_attn_wq"], np.float32),
        "wk": np.asarray(inp["L0_attn_wk"], np.float32),
        "wv": np.asarray(inp["L0_attn_wv"], np.float32),
        "wgq": np.asarray(inp["L0_attn_wgq"], np.float32),
        "wgk": np.asarray(inp["L0_attn_wgk"], np.float32),
        "wgv": np.asarray(inp["L0_attn_wgv"], np.float32),
        "rout": np.asarray(inp["L0_attn_r_out_w"], np.float32),
        "wout": np.asarray(inp["L0_attn_w_out"], np.float32),
        "glw": np.asarray(inp["L0_attn_gate_local_w"], np.float32),
        "grw": np.asarray(inp["L0_attn_gate_remote_w"], np.float32),
        "gw": np.asarray(inp["L0_ffn_gate_w"], np.float32),
        "gb": np.asarray(inp["L0_ffn_gate_b"], np.float32),
        "w1": np.asarray(inp["L0_ffn_w1"], np.float32),
        "b1": np.asarray(inp["L0_ffn_b1"], np.float32),
        "w2": np.asarray(inp["L0_ffn_w2"], np.float32),
        "b2": np.asarray(inp["L0_ffn_b2"], np.float32),
    }
    gbd = np.zeros((D, 32), np.float32)
    for h in range(H):
        rho = (h % 2) * 8 + h // 2
        gbd[h * DH:(h + 1) * DH, rho] = W["glw"][h]
        gbd[h * DH:(h + 1) * DH, 16 + rho] = W["grw"][h]
    ident = np.eye(128, dtype=ml_dtypes.bfloat16)
    exl = np.zeros((16, NDC * 128), ml_dtypes.bfloat16)
    for c in range(8):
        for dd in range(128):
            h = 2 * c + dd // 64
            exl[(h % 2) * 8 + h // 2, c * 128 + dd] = 1
    exr = np.zeros((8, 4 * 128), ml_dtypes.bfloat16)
    for c in range(4):
        for dd in range(128):
            lh = 2 * c + dd // 64
            exr[(lh % 2) * 4 + lh // 2, c * 128 + dd] = 1
    in_maps = []
    for ccc in range(NCORES):
        t0 = ccc * T
        xh = np.zeros((TH, D), np.float32)
        lo = max(0, t0 - HALO)
        xh[HALO - (t0 - lo):] = xg[lo:t0 + T]
        masks = np.zeros((NQG, 4, 128, 256), ml_dtypes.bfloat16)
        latmask = np.zeros((NQG, 128, 256), ml_dtypes.bfloat16)
        for g in range(NQG):
            qglob = t0 + g * 256 + np.arange(256)[None, :]
            for kc in range(4):
                kglob = t0 - 256 + g * 256 + kc * 128 + np.arange(128)[:, None]
                ok = (kglob >= 0) & (kglob > qglob - 256) & (kglob <= qglob)
                masks[g, kc] = ok.astype(np.float32)
            lat_ok = (np.arange(64)[:, None] <= (qglob[0][None, :] // 256))
            latmask[g, 0:64] = lat_ok.astype(np.float32)
            latmask[g, 64:128] = lat_ok.astype(np.float32)
        oh = np.zeros((128, 8), np.float32)
        oh[:, ccc] = 1.0
        in_maps.append({
            "x": xh, "wq": W["wq"], "wk": W["wk"], "wv": W["wv"],
            "wgq": W["wgq"], "wgk": W["wgk"], "wgv": W["wgv"],
            "rout": W["rout"], "wout": W["wout"], "gbd": gbd,
            "gw": W["gw"], "gbias": W["gb"].reshape(1, 8),
            "w1": W["w1"][ccc], "b1col": np.ascontiguousarray(
                W["b1"][ccc].reshape(NHC, 128).T),
            "w2": W["w2"][ccc], "b2row": W["b2"][ccc].reshape(1, D),
            "onehot": oh, "ident": ident, "exl": exl, "exr": exr,
            "masks": masks, "latmask": latmask,
        })
    return in_maps


def kernel(**inputs):
    in_maps = _host_inputs(inputs)
    nc = build()
    res = run_bass_kernel_spmd(nc, in_maps, core_ids=list(range(NCORES)))
    out = np.concatenate([np.asarray(res.results[c]["y"], dtype=np.float32)
                          for c in range(NCORES)], axis=0)
    return out.reshape(1, S, D)
